# version 08
# baseline (speedup 1.0000x reference)
import numpy as np
import ml_dtypes
from contextlib import ExitStack

import concourse.bass as bass
import concourse.mybir as mybir
from concourse.bass_utils import run_bass_kernel_spmd

F32 = mybir.dt.float32
BF16 = mybir.dt.bfloat16
AF = mybir.ActivationFunctionType
ALU = mybir.AluOpType
AX = mybir.AxisListType

T = 2048
C = 1024
NB = 2
NCH = 4
CHK = 512
NP = 8
EPS = 1e-5
ENGS = ("tensor", "vector", "scalar", "gpsimd", "sync")
DEBUG = False
SKIP_ATTN = False
TAPS = {}


def build_program():
    nc = bass.Bass("TRN2", target_bir_lowering=False)

    d_x = nc.dram_tensor("xT", [NB, C, T], F32, kind="ExternalInput").ap()
    d_out = nc.dram_tensor("outT", [NB, C, T], F32, kind="ExternalOutput").ap()
    d_wr = nc.dram_tensor("wrT", [C, C], F32, kind="ExternalInput").ap()
    d_wk = nc.dram_tensor("wkT", [C, C], F32, kind="ExternalInput").ap()
    d_wv = nc.dram_tensor("wvT", [C, C], F32, kind="ExternalInput").ap()
    d_wq = nc.dram_tensor("wqT", [C, 512], F32, kind="ExternalInput").ap()
    d_wqk = nc.dram_tensor("wqkT", [512, C], F32, kind="ExternalInput").ap()
    d_wkvr = nc.dram_tensor("wkvrT", [C, 128], F32, kind="ExternalInput").ap()
    d_wvu = nc.dram_tensor("wvuT", [64, C], F32, kind="ExternalInput").ap()
    d_wpj = nc.dram_tensor("wpjT", [C, C], F32, kind="ExternalInput").ap()
    d_mur = nc.dram_tensor("mur", [128, 8], F32, kind="ExternalInput").ap()
    d_muk = nc.dram_tensor("muk", [128, 8], F32, kind="ExternalInput").ap()
    d_bq = nc.dram_tensor("bq", [128, 4], F32, kind="ExternalInput").ap()
    d_bqk = nc.dram_tensor("bqk", [128, 8], F32, kind="ExternalInput").ap()
    d_bkvr = nc.dram_tensor("bkvr", [128, 1], F32, kind="ExternalInput").ap()
    d_bvu = nc.dram_tensor("bvu", [128, 8], F32, kind="ExternalInput").ap()
    d_bpj = nc.dram_tensor("bpj", [128, 8], F32, kind="ExternalInput").ap()
    d_idn = nc.dram_tensor("idn", [128, 128], BF16, kind="ExternalInput").ap()
    taps = {}
    if DEBUG:
        for nm, shp, dt in (("tapA", [C, T], F32), ("tapB", [C, T], F32),
                            ("tapCD", [C, T], F32), ("tapckv", [128, T], BF16),
                            ("tapq", [128, 8 * 1024], BF16),
                            ("tapyh", [128, 8 * 1024], BF16),
                            ("tapnegm", [128, 16], F32),
                            ("tapcq", [128, 4 * T], BF16),
                            ("tapxr", [128, 8 * 512], BF16),
                            ("tapxk", [128, 8 * 512], BF16),
                            ("tapr0", [128, 8 * 512], BF16),
                            ("tapka0", [128, 8 * 512], BF16),
                            ("tapxh0", [128, 8 * 513], F32),
                            ("tapx0", [128, 8 * 512], F32),
                            ("tapx1", [128, 8 * 512], F32),
                            ("tapw", [128, 8 * 1024], BF16),
                            ("tapyh0", [128, 8 * 1024], BF16)):
            taps[nm] = nc.dram_tensor(nm, shp, dt, kind="ExternalOutput").ap()

    streams = {k: [] for k in ENGS}

    with ExitStack() as ctx:
        sb = lambda n, s, d: ctx.enter_context(nc.sbuf_tensor(n, s, d))

        x_sb = sb("x_sb", [128, 8 * T], F32)
        wr_sb = sb("wr_sb", [128, 8 * 1024], BF16)
        wk_sb = sb("wk_sb", [128, 8 * 1024], BF16)
        wvs_sb = sb("wvs_sb", [128, 8 * 1024], BF16)
        wq_sb = sb("wq_sb", [128, 8 * 512], BF16)
        wqk_sb = sb("wqk_sb", [128, 4 * 1024], BF16)
        wkvr_sb = sb("wkvr_sb", [128, 8 * 128], BF16)
        wvu_sb = sb("wvu_sb", [64, 1024], BF16)
        arena = sb("arena", [128, 37392], BF16)

        mur_sb = sb("mur_sb", [128, 8], F32)
        muk_sb = sb("muk_sb", [128, 8], F32)
        bq_sb = sb("bq_sb", [128, 4], F32)
        bqk_sb = sb("bqk_sb", [128, 8], F32)
        bkvr_sb = sb("bkvr_sb", [128, 1], F32)
        bvu_bf = sb("bvu_bf", [128, 8], BF16)
        bpj_sb = sb("bpj_sb", [128, 8], F32)
        beff_sb = sb("beff_sb", [128, 8], F32)
        eps_sb = sb("eps_sb", [128, 1], F32)
        idn_sb = sb("idn_sb", [128, 128], BF16)
        ones1 = sb("ones1", [128, 128], BF16)
        onesm = sb("onesm", [128, 128], BF16)
        negm_sb = sb("negm_sb", [128, 16], F32)
        pmax2 = sb("pmax2", [128, 2], BF16)
        m2b = sb("m2b", [128, 1], BF16)
        rowb = sb("rowb", [128, 2], BF16)

        PSA = ctx.enter_context(nc.psum_tensor("PSA", [128, 1024], F32))
        PSB = ctx.enter_context(nc.psum_tensor("PSB", [128, 1024], F32))
        PSY = ctx.enter_context(nc.psum_tensor("PSY", [128, 1024], F32))
        PSM = ctx.enter_context(nc.psum_tensor("PSM", [128, 512], F32))
        PSBF = ctx.enter_context(nc.psum_tensor("PSBF", [128, 256], BF16))

        SLOT = [PSA[:, 0:512], PSA[:, 512:1024], PSB[:, 0:512],
                PSB[:, 512:1024], PSY[:, 0:512]]
        SLOT_RG = ["A0", "A1", "B0", "B1", "Y0"]
        MSQ = PSY[:, 512:1024]

        arb = arena[:]
        arf = arena[:].bitcast(F32)

        def vb(off, shape):
            n = int(np.prod(shape[1:]))
            a = arb[:, off // 2: off // 2 + n]
            if len(shape) == 3:
                a = a.rearrange("p (a b) -> p a b", a=shape[1])
            elif len(shape) == 4:
                a = a.rearrange("p (a b c) -> p a b c", a=shape[1], b=shape[2])
            return a

        def vf(off, shape):
            n = int(np.prod(shape[1:]))
            a = arf[:, off // 4: off // 4 + n]
            if len(shape) == 3:
                a = a.rearrange("p (a b) -> p a b", a=shape[1])
            return a

        xr_bf = vb(0, [128, 8, 512])
        xbf2 = vb(8192, [128, 8, 512])
        xsq = vb(16384, [128, 8, 512])
        gpd = vf(16384, [128, 512])
        mean_sb = vf(24576, [128, 512])
        var_sb = vf(26624, [128, 512])
        tmpv = vf(28672, [128, 512])
        qhalf = vb(0, [128, 8, 1024])
        vaug = vb(16384, [128, 16, 256])
        vaug4 = vb(16384, [128, 16, 4, 64])
        expA = vb(24576, [128, 2, 1024])
        xhat = vf(30720, [128, 8, 513])
        xk_bf = vb(47136, [128, 8, 512])
        kact = vb(55328, [128, 8, 512])
        r_bf = vb(63520, [128, 8, 512])
        dbuf = vf(71712, [128, 512])
        relu2 = vb(73760, [128, 512])
        cq_bf = vb(30720, [128, 4, 2048])
        ckv = vb(47104, [128, 2048])
        yhat = vb(51200, [128, 8, 1024])
        expB = vb(67584, [128, 2, 1024])
        recmv = vb(71680, [128, 512])

        x3 = x_sb[:].rearrange("p (ct t) -> p ct t", ct=8)
        wrv = wr_sb[:].rearrange("p (k d) -> p k d", k=8)
        wkv_ = wk_sb[:].rearrange("p (k d) -> p k d", k=8)
        wvsv = wvs_sb[:].rearrange("p (k d) -> p k d", k=8)
        wqv = wq_sb[:].rearrange("p (k d) -> p k d", k=8)
        wqkv = wqk_sb[:].rearrange("p (k d) -> p k d", k=4)
        wkvrv = wkvr_sb[:].rearrange("p (k d) -> p k d", k=8)

        class Sem:
            def __init__(self, name):
                self.h = ctx.enter_context(nc.semaphore(name))
                self.n = 0

        ST, SS, SV, SG = Sem("ST"), Sem("SS"), Sem("SV"), Sem("SG")
        SDW, SDX, SO = Sem("SDW"), Sem("SDX"), Sem("SO")

        def op(eng, build, sem=None, k=1):
            if sem is None:
                streams[eng].append(build)
                return None
            sem.n += k
            streams[eng].append(
                lambda e, b=build, hh=sem.h, kv=k: b(e).then_inc(hh, kv))
            return sem.n

        def wt(eng, sem, val):
            if sem is None or val is None or val <= 0:
                return
            streams[eng].append(lambda e, hh=sem.h, v=val: e.wait_ge(hh, v))

        FREE = {r: (SV, 0) for r in
                ("A0", "A1", "B0", "B1", "Y0", "Y1", "M", "BF")}

        def waitfree(eng, *regions):
            for r in regions:
                s, v = FREE[r]
                wt(eng, s, v)

        def setfree(region, sem, val):
            FREE[region] = (sem, val)

        xin_ch = [[0] * NCH for _ in range(NB)]

        def emit_xload(b, ch):
            src = d_x[b].rearrange("(ct p) t -> p ct t", p=128)[
                :, :, ch * CHK:(ch + 1) * CHK]
            dst = x3[:, :, ch * CHK:(ch + 1) * CHK]
            xin_ch[b][ch] = op("sync",
                               lambda e, s=src, d=dst: e.dma_start(out=d, in_=s),
                               SDX, 16)

        for ch in range(NCH):
            emit_xload(0, ch)
        for dst, src in ((mur_sb, d_mur), (muk_sb, d_muk), (bq_sb, d_bq),
                         (bqk_sb, d_bqk), (bkvr_sb, d_bkvr), (bpj_sb, d_bpj),
                         (idn_sb, d_idn)):
            op("sync", lambda e, d=dst, s=src: e.dma_start(out=d[:], in_=s),
               SDX, 16)
        sdx_smalls = SDX.n

        wdma = {}

        def emit_wload(name, dst, src):
            wdma[name] = op("gpsimd",
                            lambda e, d=dst, s=src: e.dma_start(out=d, in_=s),
                            SDW, 16)

        emit_wload("wr", wrv, d_wr.rearrange("(k p) d -> p k d", p=128))
        emit_wload("wk", wkv_, d_wk.rearrange("(k p) d -> p k d", p=128))
        emit_wload("wv0", wvsv, d_wv.rearrange("(k p) d -> p k d", p=128))
        emit_wload("wkvr", wkvrv, d_wkvr.rearrange("(k p) d -> p k d", p=128))
        emit_wload("wq", wqv, d_wq.rearrange("(k p) d -> p k d", p=128))
        emit_wload("wqk", wqkv, d_wqk.rearrange("(k p) d -> p k d", p=128))
        emit_wload("wvu", wvu_sb[:], d_wvu)
        emit_wload("bvu", bvu_bf[:], d_bvu)

        op("vector", lambda e: e.memset(ones1[:], 1.0))
        op("vector", lambda e: e.memset(onesm[:], 1.0 / 1024.0))
        op("vector", lambda e: e.memset(eps_sb[:], EPS))
        op("vector", lambda e: e.memset(m2b[:], 0.0))
        op("vector", lambda e: e.memset(pmax2[:], 0.0))
        v_ones = op("vector", lambda e: e.memset(m2b[:], 0.0), SV)
        for _eng in ("tensor", "scalar", "gpsimd"):
            wt(_eng, SV, v_ones)
            wt(_eng, SDX, sdx_smalls)
        wt("vector", SDX, sdx_smalls)

        stats_gate = [0, 0]
        x_upd = {}
        wv_reload = [0]
        si = [0]
        slot_free = [(SV, 0)] * 5
        probe_prev = {"t1": 0, "t2": 0, "t3": 0}
        negm_gate = [0]
        xsq_gate = [0]
        psm_recmv_cons = [0]
        last_tmv = [0]
        kvev = [0] * 8

        def stats_stage(b, ch, xgate):
            cc = slice(ch * CHK, (ch + 1) * CHK)
            wt("gpsimd", SDX, xin_ch[b][ch])
            if xgate:
                wt("gpsimd", xgate[0], xgate[1])
            wt("gpsimd", ST, stats_gate[0])
            g_cast = op("gpsimd", lambda e, c=cc: e.tensor_copy(
                out=xbf2[:, :, :], in_=x3[:, :, c]), SG)
            wt("scalar", SDX, xin_ch[b][ch])
            if xgate:
                wt("scalar", xgate[0], xgate[1])
            wt("scalar", ST, stats_gate[1])
            wt("scalar", SV, xsq_gate[0])
            s_xsq = op("scalar", lambda e, c=cc: e.activation(
                out=xsq[:, :, :], in_=x3[:, :, c], func=AF.Square), SS)

            wt("tensor", SG, g_cast)
            wt("tensor", SS, s_xsq)
            waitfree("tensor", "M", "Y1")
            t_mean = None
            for k in range(8):
                t_mean = op("tensor", lambda e, kk=k: e.matmul(
                    PSM[:], onesm[:], xbf2[:, kk, :],
                    start=(kk == 0), stop=(kk == 7)), ST if k == 7 else None)
            t_msq = None
            for k in range(8):
                t_msq = op("tensor", lambda e, kk=k: e.matmul(
                    MSQ, onesm[:], xsq[:, kk, :],
                    start=(kk == 0), stop=(kk == 7)), ST if k == 7 else None)
            stats_gate[0] = t_mean
            stats_gate[1] = t_msq

            wt("scalar", ST, t_mean)
            s_mean = op("scalar", lambda e: e.activation(
                out=mean_sb[:], in_=PSM[:], func=AF.Identity), SS)
            setfree("M", SS, s_mean)
            s_m2 = op("scalar", lambda e: e.activation(
                out=var_sb[:], in_=mean_sb[:], func=AF.Square), SS)
            wt("vector", ST, t_msq)
            wt("vector", SS, s_m2)
            v_var = op("vector", lambda e: e.tensor_tensor(
                out=var_sb[:], in0=MSQ, in1=var_sb[:], op=ALU.subtract), SV)
            setfree("Y1", SV, v_var)
            wt("scalar", SV, v_var)
            op("scalar", lambda e: e.activation(
                out=var_sb[:], in_=var_sb[:], func=AF.Ln,
                bias=eps_sb[:, 0:1], scale=1.0))
            s_rstd = op("scalar", lambda e: e.activation(
                out=var_sb[:], in_=var_sb[:], func=AF.Exp,
                bias=0.0, scale=-0.5), SS)
            return dict(rstd=s_rstd, cc=cc)

        VCT, GCT = (0, 1, 2, 3), (4, 5, 6, 7)
        gpd_ct = {c: vf(16384 + 2048 * i, [128, 512]) for i, c in enumerate(GCT)}

        def mix_stage(b, ch, st, r_gate, k_gate):
            cc = st["cc"]
            wt("gpsimd", SS, st["rstd"])
            wt("gpsimd", ST, max(r_gate or 0, k_gate or 0))
            wt("gpsimd", SV, xsq_gate[0])
            g_d = {}
            for ct in GCT:
                if ch == 0:
                    op("gpsimd", lambda e, c=ct: e.memset(xhat[:, c, 0:1], 0.0))
                else:
                    op("gpsimd", lambda e, c=ct: e.tensor_copy(
                        out=xhat[:, c, 0:1], in_=xhat[:, c, 512:513]))
                op("gpsimd", lambda e, c=ct, s=cc: e.tensor_tensor(
                    out=xhat[:, c, 1:513], in0=x3[:, c, s], in1=mean_sb[:],
                    op=ALU.subtract))
                op("gpsimd", lambda e, c=ct: e.tensor_tensor(
                    out=xhat[:, c, 1:513], in0=xhat[:, c, 1:513],
                    in1=var_sb[:], op=ALU.mult))
                g_d[ct] = op("gpsimd", lambda e, c=ct: e.tensor_tensor(
                    out=gpd_ct[c][:], in0=xhat[:, c, 1:513],
                    in1=xhat[:, c, 0:512], op=ALU.subtract), SG)
            wt("vector", SS, st["rstd"])
            wt("vector", ST, r_gate)
            wt("vector", ST, k_gate)
            v_last = None
            for ct in VCT:
                if ch == 0:
                    op("vector", lambda e, c=ct: e.memset(xhat[:, c, 0:1], 0.0))
                else:
                    op("vector", lambda e, c=ct: e.tensor_copy(
                        out=xhat[:, c, 0:1], in_=xhat[:, c, 512:513]))
                op("vector", lambda e, c=ct, s=cc: e.tensor_tensor(
                    out=xhat[:, c, 1:513], in0=x3[:, c, s], in1=mean_sb[:],
                    op=ALU.subtract))
                op("vector", lambda e, c=ct: e.tensor_tensor(
                    out=xhat[:, c, 1:513], in0=xhat[:, c, 1:513],
                    in1=var_sb[:], op=ALU.mult))
                op("vector", lambda e, c=ct: e.tensor_tensor(
                    out=dbuf[:], in0=xhat[:, c, 1:513], in1=xhat[:, c, 0:512],
                    op=ALU.subtract))
                op("vector", lambda e, c=ct: e.scalar_tensor_tensor(
                    out=xr_bf[:, c, :], in0=dbuf[:], scalar=mur_sb[:, c:c + 1],
                    in1=xhat[:, c, 0:512], op0=ALU.mult, op1=ALU.add))
                v_last = op("vector", lambda e, c=ct: e.scalar_tensor_tensor(
                    out=xk_bf[:, c, :], in0=dbuf[:], scalar=muk_sb[:, c:c + 1],
                    in1=xhat[:, c, 0:512], op0=ALU.mult, op1=ALU.add), SV)
            for ct in GCT:
                wt("vector", SG, g_d[ct])
                op("vector", lambda e, c=ct: e.scalar_tensor_tensor(
                    out=xr_bf[:, c, :], in0=gpd_ct[c][:],
                    scalar=mur_sb[:, c:c + 1], in1=xhat[:, c, 0:512],
                    op0=ALU.mult, op1=ALU.add))
                v_last = op("vector", lambda e, c=ct: e.scalar_tensor_tensor(
                    out=xk_bf[:, c, :], in0=gpd_ct[c][:],
                    scalar=muk_sb[:, c:c + 1], in1=xhat[:, c, 0:512],
                    op0=ALU.mult, op1=ALU.add), SV)
            xsq_gate[0] = v_last
            return {"vector": v_last, "gpsimd": g_d[GCT[-1]]}

        def mm_group(w_view, rhs_view, m, wgate=None):
            s = si[0] % 5
            si[0] += 1
            sl = SLOT[s]
            waitfree("tensor", SLOT_RG[s])
            if wgate:
                wt("tensor", SDW, wgate)
            t = None
            for k in range(8):
                t = op("tensor", lambda e, kk=k, sl_=sl, w=w_view, r=rhs_view,
                       mm=m: e.matmul(sl_, w[:, kk, mm * 128:(mm + 1) * 128],
                                      r[:, kk, :], start=(kk == 0),
                                      stop=(kk == 7)), ST if k == 7 else None)
            return s, sl, t

        def ffn_phase(b, is_E):
            ph = "E" if is_E else "A"
            gate_all = ST.n
            stats_gate[0] = max(stats_gate[0], gate_all)
            stats_gate[1] = max(stats_gate[1], gate_all)

            def xgate(ch):
                if is_E:
                    if SKIP_ATTN:
                        return x_upd.get((b, "A", ch))
                    return x_upd.get((b, "D", ch))
                return None

            st = stats_stage(b, 0, xgate(0))
            mix = mix_stage(b, 0, st, gate_all, gate_all)
            r_last = [gate_all]
            k_last = [gate_all]
            for ch in range(NCH):
                cc = slice(ch * CHK, (ch + 1) * CHK)
                wt("tensor", SV, mix["vector"])
                wt("tensor", SG, mix["gpsimd"])
                t_r = None
                for m in range(8):
                    s, sl, t = mm_group(wrv, xr_bf, m,
                                        wdma["wr"] if (b == 0 and not is_E and
                                                       ch == 0) else None)
                    t_r = t
                    wt("scalar", ST, t)
                    wt("scalar", SV, kvev[m])
                    c = op("scalar", lambda e, sl_=sl, mm=m: e.activation(
                        out=r_bf[:, mm, :], in_=sl_, func=AF.Sigmoid), SS)
                    setfree(SLOT_RG[s], SS, c)
                r_last[0] = t_r
                if ch + 1 < NCH:
                    st2 = stats_stage(b, ch + 1, xgate(ch + 1))
                t_k = None
                for m in range(8):
                    s, sl, t = mm_group(wkv_, xk_bf, m,
                                        wdma["wk"] if (b == 0 and not is_E and
                                                       ch == 0) else None)
                    t_k = t
                    wt("scalar", ST, t)
                    c = op("scalar", lambda e, sl_=sl, mm=m: e.activation(
                        out=kact[:, mm, :], in_=sl_, func=AF.Relu), SS)
                    wt("vector", SS, c)
                    cv = op("vector", lambda e, sl_=sl, mm=m:
                            e.tensor_tensor(out=kact[:, mm, :],
                                            in0=kact[:, mm, :],
                                            in1=sl_, op=ALU.mult), SV)
                    setfree(SLOT_RG[s], SV, cv)
                k_last[0] = t_k
                xdone = None
                for m in range(8):
                    wg = None
                    if is_E and ch == 0 and m == 0:
                        wg = wv_reload[0]
                    elif b == 0 and not is_E and ch == 0:
                        wg = wdma["wv0"]
                    s, sl, t = mm_group(wvsv, kact, m, wg)
                    wt("vector", ST, t)
                    op("vector", lambda e, sl_=sl, mm=m: e.scalar_tensor_tensor(
                        out=tmpv[:], in0=sl_, scalar=0.5, in1=r_bf[:, mm, :],
                        op0=ALU.mult, op1=ALU.mult))
                    cv = op("vector", lambda e, mm=m, c2=cc: e.tensor_tensor(
                        out=x3[:, mm, c2], in0=x3[:, mm, c2], in1=tmpv[:],
                        op=ALU.add), SV)
                    setfree(SLOT_RG[s], SV, cv)
                    kvev[m] = cv
                    xdone = cv
                x_upd[(b, ph, ch)] = (SV, xdone)
                if DEBUG and b == 0 and not is_E and ch == 0:
                    wt("sync", SV, xdone)
                    op("sync", lambda e: e.dma_start(out=taps["tapr0"], in_=r_bf), SO, 16)
                    op("sync", lambda e: e.dma_start(out=taps["tapka0"], in_=kact), SO, 16)
                    op("sync", lambda e: e.dma_start(out=taps["tapxh0"], in_=xhat), SO, 16)
                    op("sync", lambda e: e.dma_start(out=taps["tapx0"], in_=x3[:, :, 0:512]), SO, 16)
                if DEBUG and b == 0 and not is_E and ch == 1:
                    wt("sync", SV, xdone)
                    op("sync", lambda e: e.dma_start(out=taps["tapx1"], in_=x3[:, :, 512:1024]), SO, 16)
                if is_E:
                    wt("sync", SV, xdone)
                    so = op("sync", lambda e, bb=b, c2=cc, c=ch: e.dma_start(
                        out=d_out[bb].rearrange("(ct p) t -> p ct t", p=128)[
                            :, :, c * CHK:(c + 1) * CHK],
                        in_=x3[:, :, c2]), SO, 16)
                    if b == 0:
                        wt("sync", SO, so)
                        emit_xload(1, ch)
                if ch + 1 < NCH:
                    mix = mix_stage(b, ch + 1, st2, r_last[0], k_last[0])
                    st = st2

        def phase_b(b):
            wt("gpsimd", ST, ST.n)
            wdma["wpj"] = op("gpsimd", lambda e: e.dma_start(
                out=wvsv, in_=d_wpj.rearrange("(k p) d -> p k d", p=128)),
                SDW, 16)
            prev_ck = 0
            for ch in range(NCH):
                cc = slice(ch * CHK, (ch + 1) * CHK)
                st = stats_stage(b, ch, x_upd[(b, "A", ch)])
                wt("vector", SS, st["rstd"])
                wt("vector", ST, prev_ck)
                xhb = None
                for ct in range(8):
                    op("vector", lambda e, c=ct, s=cc: e.tensor_tensor(
                        out=tmpv[:], in0=x3[:, c, s], in1=mean_sb[:],
                        op=ALU.subtract))
                    xhb = op("vector", lambda e, c=ct: e.tensor_tensor(
                        out=xr_bf[:, c, :], in0=tmpv[:], in1=var_sb[:],
                        op=ALU.mult), SV)
                wt("tensor", SV, xhb)
                for mq in range(4):
                    waitfree("tensor", SLOT_RG[mq])
                    if b == 0 and ch == 0 and mq == 0:
                        wt("tensor", SDW, wdma["wq"])
                    t = None
                    for k in range(8):
                        t = op("tensor", lambda e, kk=k, m=mq: e.matmul(
                            SLOT[m], wqv[:, kk, m * 128:(m + 1) * 128],
                            xr_bf[:, kk, :], start=(kk == 0), stop=(kk == 7)),
                            ST if k == 7 else None)
                    wt("scalar", ST, t)
                    c = op("scalar", lambda e, m=mq, s=cc: e.activation(
                        out=cq_bf[:, m, s], in_=SLOT[m], func=AF.Identity,
                        bias=bq_sb[:, m:m + 1], scale=1.0), SS)
                    setfree(SLOT_RG[mq], SS, c)
                waitfree("tensor", "Y0")
                if b == 0 and ch == 0:
                    wt("tensor", SDW, wdma["wkvr"])
                t = None
                for k in range(8):
                    t = op("tensor", lambda e, kk=k: e.matmul(
                        SLOT[4], wkvrv[:, kk, :], xr_bf[:, kk, :],
                        start=(kk == 0), stop=(kk == 7)), ST if k == 7 else None)
                prev_ck = t
                wt("scalar", ST, t)
                c = op("scalar", lambda e, s=cc: e.activation(
                    out=ckv[:, s], in_=SLOT[4], func=AF.Identity,
                    bias=bkvr_sb[:, 0:1], scale=1.0), SS)
                setfree("Y0", SS, c)
            if b == 0:
                wt("tensor", SDW, wdma["wpj"])
                wt("tensor", SDW, wdma["bvu"])
                waitfree("tensor", "M")
                t = None
                for m in range(8):
                    for k in range(8):
                        t = op("tensor", lambda e, kk=k, mm=m: e.matmul(
                            PSM[:, mm:mm + 1],
                            wvsv[:, kk, mm * 128:(mm + 1) * 128],
                            bvu_bf[:, kk:kk + 1], start=(kk == 0),
                            stop=(kk == 7)),
                            ST if (k == 7 and m == 7) else None)
                wt("vector", ST, t)
                v = op("vector", lambda e: e.tensor_tensor(
                    out=beff_sb[:], in0=PSM[:, 0:8], in1=bpj_sb[:],
                    op=ALU.add), SV)
                setfree("M", SV, v)
                return v
            return None

        def probe_chain(p, half):
            waitfree("tensor", "M")
            wt("tensor", ST, probe_prev["t1"])
            tA = op("tensor", lambda e, pp=p, h=half: e.matmul(
                PSM[:], ckv[0:64, 0:128],
                qhalf[0:64, pp, 0:512],
                start=True, stop=True, tile_position=(0, 0)), ST)
            wt("vector", ST, tA)
            vA = op("vector", lambda e: e.reduce_max(
                out=pmax2[:, 0:1], in_=PSM[:], axis=AX.X), SV)
            wt("tensor", SV, vA)
            tB = op("tensor", lambda e, pp=p: e.matmul(
                PSM[:], ckv[64:128, 0:128], qhalf[64:128, pp, 0:512],
                start=True, stop=True, tile_position=(64, 0)), ST)
            wt("vector", ST, tB)
            vB = op("vector", lambda e: e.reduce_max(
                out=pmax2[:, 1:2], in_=PSM[:], axis=AX.X), SV)
            waitfree("tensor", "BF")
            wt("tensor", SV, vB)
            t1 = op("tensor", lambda e: e.transpose(
                out=PSBF[0:2, 0:128], in_=pmax2[:], identity=idn_sb[:]), ST)
            probe_prev["t1"] = t1
            wt("vector", ST, t1)
            wt("vector", ST, probe_prev["t2"])
            v2 = op("vector", lambda e: e.reduce_max(
                out=m2b[0:2, 0:1], in_=PSBF[0:2, 0:128], axis=AX.X), SV)
            wt("tensor", SV, v2)
            t2 = op("tensor", lambda e: e.transpose(
                out=PSBF[0:1, 128:256], in_=m2b[:], identity=idn_sb[:]), ST)
            probe_prev["t2"] = t2
            wt("scalar", ST, t2)
            wt("scalar", ST, probe_prev["t3"])
            s1 = op("scalar", lambda e: e.activation(
                out=rowb[0:1, 0:2], in_=PSBF[0:1, 128:130],
                func=AF.Identity), SS)
            setfree("BF", SS, s1)
            wt("tensor", SS, s1)
            t3 = op("tensor", lambda e: e.matmul(
                PSM[:, 0:2], ones1[0:1, :], rowb[0:1, 0:2],
                start=True, stop=True), ST)
            probe_prev["t3"] = t3
            wt("vector", ST, t3)
            wt("vector", SS, negm_gate[0])
            v3 = op("vector", lambda e, pp=p: e.tensor_scalar(
                out=negm_sb[:, 2 * pp:2 * pp + 2], in0=PSM[:, 0:2],
                scalar1=-0.125, scalar2=None, op0=ALU.mult), SV)
            setfree("M", SV, v3)
            return v3

        def phase_cd(b, beff_done):
            negm_ready = [0] * NP
            negm_gate[0] = SS.n
            cq_ready = SS.n
            for half in range(2):
                hoff = half * 1024
                if b == 0 and half == 0:
                    wt("tensor", SDW, wdma["wqk"])
                wt("tensor", SS, cq_ready)
                qdone = None
                for p in range(NP):
                    sidx = p % 4
                    for tb2 in range(2):
                        waitfree("tensor", SLOT_RG[sidx])
                        t = None
                        for k in range(4):
                            t = op("tensor", lambda e, kk=k, pp=p, t2=tb2, h2=hoff:
                                   e.matmul(SLOT[pp % 4],
                                            wqkv[:, kk, pp * 128:(pp + 1) * 128],
                                            cq_bf[:, kk, h2 + t2 * 512:
                                                  h2 + (t2 + 1) * 512],
                                            start=(kk == 0), stop=(kk == 3)),
                                   ST if k == 3 else None)
                        wt("scalar", ST, t)
                        qdone = op("scalar", lambda e, pp=p, t2=tb2: e.activation(
                            out=qhalf[:, pp, t2 * 512:(t2 + 1) * 512],
                            in_=SLOT[pp % 4], func=AF.Identity,
                            bias=bqk_sb[:, pp:pp + 1], scale=1.0), SS)
                        setfree(SLOT_RG[sidx], SS, qdone)
                if half == 0:
                    wt("tensor", SS, qdone)
                    negm_ready[0] = probe_chain(0, half)
                expA_free = [(ST, 0), (ST, 0)]
                expB_free = [(ST, 0), (ST, 0)]
                for p in range(NP):
                    vevac = None
                    for rnd in range(2):
                        waitfree("tensor", "Y0", "Y1")
                        if rnd == 0:
                            wt("tensor", SV, v_ones)
                            if b == 0 and half == 0 and p == 0:
                                wt("tensor", SDW, wdma["wvu"])
                        else:
                            wt("tensor", SV, vevac)
                        t = None
                        for s8 in range(8):
                            stile = rnd * 8 + s8
                            t = op("tensor", lambda e, s_=s8, sl_=stile, pp=p:
                                   e.matmul(PSY[:, s_ * 128:(s_ + 1) * 128],
                                            ckv[0:64, sl_ * 128:(sl_ + 1) * 128],
                                            wvu_sb[0:64, pp * 128:(pp + 1) * 128],
                                            start=True, stop=True),
                                   ST if s8 == 7 else None)
                        wt("vector", ST, t)
                        psyv = PSY[:].rearrange("p (s g c) -> p s g c", s=8, g=2)
                        op("vector", lambda e, r=rnd: e.memset(
                            vaug4[:, r * 8:(r + 1) * 8, 1:3, :], 1.0))
                        op("vector", lambda e, r=rnd, pv=psyv: e.tensor_copy(
                            out=vaug4[:, r * 8:(r + 1) * 8, 0, :],
                            in_=pv[:, :, 0, :]))
                        vevac = op("vector", lambda e, r=rnd, pv=psyv:
                                   e.tensor_copy(
                                       out=vaug4[:, r * 8:(r + 1) * 8, 3, :],
                                       in_=pv[:, :, 1, :]), SV)
                    vj_done = vevac
                    setfree("Y0", SV, vj_done)
                    setfree("Y1", SV, vj_done)
                    if half == 0 and p + 1 < NP:
                        negm_ready[p + 1] = probe_chain(p + 1, half)
                    for tb in range(2):
                        t0 = tb * 512
                        att_last = None
                        for st in range(16):
                            par = st % 2
                            co = par * 512
                            rg_a = "A0" if par == 0 else "A1"
                            rg_b = "B0" if par == 0 else "B1"
                            waitfree("tensor", rg_a)
                            tscA = op("tensor", lambda e, s_=st, c_=co, pp=p,
                                      tt=t0: e.matmul(
                                          PSA[:, c_:c_ + 512],
                                          ckv[0:64, s_ * 128:(s_ + 1) * 128],
                                          qhalf[0:64, pp, tt:tt + 512],
                                          start=True, stop=True,
                                          tile_position=(0, 0)), ST)
                            waitfree("tensor", rg_b)
                            tscB = op("tensor", lambda e, s_=st, c_=co, pp=p,
                                      tt=t0: e.matmul(
                                          PSB[:, c_:c_ + 512],
                                          ckv[64:128, s_ * 128:(s_ + 1) * 128],
                                          qhalf[64:128, pp, tt:tt + 512],
                                          start=True, stop=True,
                                          tile_position=(64, 0)), ST)
                            wt("scalar", ST, tscA)
                            wt("scalar", SV, negm_ready[p])
                            wt("scalar", expA_free[par][0], expA_free[par][1])
                            eA = op("scalar", lambda e, pa=par, c_=co, pp=p:
                                    e.activation(out=expA[:, pa, 0:512],
                                                 in_=PSA[:, c_:c_ + 512],
                                                 func=AF.Exp,
                                                 bias=negm_sb[:, 2 * pp:2 * pp + 1],
                                                 scale=0.125), SS)
                            setfree(rg_a, SS, eA)
                            wt("scalar", ST, tscB)
                            wt("scalar", expB_free[par][0], expB_free[par][1])
                            eB = op("scalar", lambda e, pa=par, c_=co, pp=p:
                                    e.activation(out=expB[:, pa, 0:512],
                                                 in_=PSB[:, c_:c_ + 512],
                                                 func=AF.Exp,
                                                 bias=negm_sb[:, 2 * pp + 1:
                                                              2 * pp + 2],
                                                 scale=0.125), SS)
                            setfree(rg_b, SS, eB)
                            if st == 0:
                                wt("tensor", SV, vj_done)
                                waitfree("tensor", "Y0", "Y1")
                            wt("tensor", SS, eA)
                            aA = op("tensor", lambda e, s_=st, pa=par:
                                    e.matmul(PSY[:, 0:512],
                                             vaug[:, s_, 0:128],
                                             expA[:, pa, 0:512],
                                             start=(s_ == 0), stop=(s_ == 15)),
                                    ST)
                            expA_free[par] = (ST, aA)
                            wt("tensor", SS, eB)
                            att_last = op("tensor", lambda e, s_=st, pa=par:
                                          e.matmul(PSY[:, 512:1024],
                                                   vaug[:, s_, 128:256],
                                                   expB[:, pa, 0:512],
                                                   start=(s_ == 0),
                                                   stop=(s_ == 15)), ST)
                            expB_free[par] = (ST, att_last)
                        wt("scalar", ST, att_last)
                        waitfree("scalar", "M")
                        op("scalar", lambda e: e.activation(
                            out=tmpv[64:128, :], in_=PSY[64:128, 0:512],
                            func=AF.Ln))
                        s_ln = op("scalar", lambda e: e.activation(
                            out=tmpv[0:64, :], in_=PSY[0:64, 512:1024],
                            func=AF.Ln), SS)
                        wt("scalar", ST, last_tmv[0])
                        wt("scalar", SV, psm_recmv_cons[0])
                        s_rec = op("scalar", lambda e: e.activation(
                            out=recmv[:], in_=tmpv[:], func=AF.Exp,
                            bias=0.0, scale=-1.0), SS)
                        waitfree("tensor", "M")
                        wt("tensor", SS, s_rec)
                        op("tensor", lambda e: e.matmul(
                            PSM[0:64, :], idn_sb[64:128, 64:128],
                            recmv[64:128, :], start=True, stop=True,
                            tile_position=(64, 0)))
                        t_mv = op("tensor", lambda e: e.matmul(
                            PSM[64:128, :], idn_sb[0:64, 0:64], recmv[0:64, :],
                            start=True, stop=True, tile_position=(0, 64)), ST)
                        last_tmv[0] = t_mv
                        wt("vector", ST, t_mv)
                        v_rc = op("vector", lambda e: e.tensor_copy(
                            out=recmv[:], in_=PSM[:]), SV)
                        setfree("M", SV, v_rc)
                        op("vector", lambda e, pp=p, tt=t0: e.tensor_tensor(
                            out=yhat[0:64, pp, tt:tt + 512],
                            in0=PSY[0:64, 0:512], in1=recmv[0:64, :],
                            op=ALU.mult))
                        v_yh = op("vector", lambda e, pp=p, tt=t0:
                                  e.tensor_tensor(
                                      out=yhat[64:128, pp, tt:tt + 512],
                                      in0=PSY[64:128, 512:1024],
                                      in1=recmv[64:128, :], op=ALU.mult), SV)
                        psm_recmv_cons[0] = v_yh
                        setfree("Y0", SV, v_yh)
                        setfree("Y1", SV, v_yh)
                yh_all = SV.n
                if DEBUG and b == 0 and half == 0:
                    wt("sync", SV, yh_all)
                    op("sync", lambda e: e.dma_start(out=taps["tapw"], in_=wvs_sb[:]), SO, 16)
                    op("sync", lambda e: e.dma_start(out=taps["tapyh0"], in_=yhat), SO, 16)
                first = True
                for tb in range(2):
                    for m in range(8):
                        s = (tb * 8 + m) % 4
                        waitfree("tensor", SLOT_RG[s])
                        if first:
                            wt("tensor", SV, yh_all)
                            if beff_done:
                                wt("tensor", SV, beff_done)
                            first = False
                        t = None
                        for k in range(8):
                            t = op("tensor", lambda e, kk=k, mm=m, t2=tb,
                                   ss=s: e.matmul(
                                       SLOT[ss],
                                       wvsv[:, kk, mm * 128:(mm + 1) * 128],
                                       yhat[:, kk, t2 * 512:(t2 + 1) * 512],
                                       start=(kk == 0), stop=(kk == 7)),
                                   ST if k == 7 else None)
                        wt("vector", ST, t)
                        cv = op("vector", lambda e, mm=m, t2=tb, ss=s, h2=hoff:
                                e.scalar_tensor_tensor(
                                    out=x3[:, mm, h2 + t2 * 512:
                                           h2 + (t2 + 1) * 512],
                                    in0=SLOT[ss],
                                    scalar=beff_sb[:, mm:mm + 1],
                                    in1=x3[:, mm, h2 + t2 * 512:
                                           h2 + (t2 + 1) * 512],
                                    op0=ALU.add, op1=ALU.add), SV)
                        setfree(SLOT_RG[s], SV, cv)
                        x_upd[(b, "D", 2 * half + tb)] = (SV, cv)
            wt("gpsimd", ST, ST.n)
            wv_reload[0] = op("gpsimd", lambda e: e.dma_start(
                out=wvsv, in_=d_wv.rearrange("(k p) d -> p k d", p=128)),
                SDW, 16)

        def tap_dump(nm, src_ap, gates):
            if not DEBUG:
                return
            for g in gates:
                if g:
                    wt("sync", g[0], g[1])
            dst = taps[nm]
            if nm in ("tapA", "tapB", "tapCD"):
                dst = dst.rearrange("(ct p) t -> p ct t", p=128)
            op("sync", lambda e, d=dst, s2=src_ap: e.dma_start(out=d, in_=s2),
               SO, 16)

        for b in range(NB):
            ffn_phase(b, is_E=False)
            if b == 0:
                tap_dump("tapA", x3, [x_upd[(0, "A", 3)]])
                tap_dump("tapxr", xr_bf.rearrange("p a b -> p (a b)") if False else xr_bf,
                         [(SV, SV.n)])
                tap_dump("tapxk", xk_bf, [(SV, SV.n)])
            if not SKIP_ATTN:
                beff_done = phase_b(b)
            if DEBUG and b == 0 and not SKIP_ATTN:
                tap_dump("tapB", x3, [(SS, SS.n), (SV, SV.n)])
                tap_dump("tapckv", ckv, [(SS, SS.n)])
                tap_dump("tapcq", cq_bf, [(SS, SS.n)])
            if not SKIP_ATTN:
                phase_cd(b, beff_done)
            if b == 0:
                tap_dump("tapCD", x3, [(SV, SV.n)])
                tap_dump("tapnegm", negm_sb[:], [(SV, SV.n)])
                tap_dump("tapyh", yhat, [(SV, SV.n)])
                tap_dump("tapq", qhalf, [(SS, SS.n)])
            ffn_phase(b, is_E=True)

        wt("sync", SO, SO.n)

        with nc.Block() as block:
            @block.sync
            def _(e):
                for fn in streams["sync"]:
                    fn(e)

            @block.tensor
            def _(e):
                for fn in streams["tensor"]:
                    fn(e)

            @block.vector
            def _(e):
                for fn in streams["vector"]:
                    fn(e)

            @block.scalar
            def _(e):
                for fn in streams["scalar"]:
                    fn(e)

            @block.gpsimd
            def _(e):
                for fn in streams["gpsimd"]:
                    fn(e)

    return nc


def kernel(**inputs):
    g = {k: np.asarray(v) for k, v in inputs.items()}
    x = g["L0_attn_x"]
    xT = np.ascontiguousarray(x.transpose(0, 2, 1)).astype(np.float32, copy=False)
    wkv = g["L0_attn_wkv_d"]
    wkvrT = np.ascontiguousarray(np.concatenate([wkv.T, wkv.T], axis=1))
    bkvr = np.ascontiguousarray(
        np.concatenate([g["L0_attn_bkv_d"], g["L0_attn_bkv_d"]]))
    common = dict(
        wrT=np.ascontiguousarray(g["L0_ffn_W_r"].T),
        wkT=np.ascontiguousarray(g["L0_ffn_W_k"].T),
        wvT=np.ascontiguousarray(g["L0_ffn_W_v"].T),
        wqT=np.ascontiguousarray(g["L0_attn_wq_d"].T),
        wqkT=np.ascontiguousarray(g["L0_attn_w_qk"].T),
        wkvrT=wkvrT,
        wvuT=np.ascontiguousarray(g["L0_attn_wv_u"].T),
        wpjT=np.ascontiguousarray(g["L0_attn_w_proj"].T),
        mur=np.ascontiguousarray(g["L0_ffn_mu_r"].reshape(8, 128).T),
        muk=np.ascontiguousarray(g["L0_ffn_mu_k"].reshape(8, 128).T),
        bq=np.ascontiguousarray(g["L0_attn_bq_d"].reshape(4, 128).T),
        bqk=np.ascontiguousarray(g["L0_attn_b_qk"].reshape(8, 128).T),
        bkvr=np.ascontiguousarray(bkvr.reshape(1, 128).T),
        bvu=np.ascontiguousarray(g["L0_attn_bv_u"].reshape(8, 128).T),
        bpj=np.ascontiguousarray(g["L0_attn_b_proj"].reshape(8, 128).T),
        idn=np.eye(128).astype(ml_dtypes.bfloat16),
    )
    in_maps = [dict(common, xT=np.ascontiguousarray(xT[2 * i:2 * i + 2]))
               for i in range(8)]
    nc = build_program()
    res = run_bass_kernel_spmd(nc, in_maps, core_ids=list(range(8)))
    if DEBUG:
        global TAPS
        TAPS = {k: np.asarray(v) for k, v in res.results[0].items()}
    out = np.concatenate([r["outT"] for r in res.results], axis=0)
    return np.ascontiguousarray(out.transpose(0, 2, 1))


if __name__ == "__main__":
    import reference as R
    ins = R.setup_inputs()
    out = kernel(**{k: np.asarray(v) for k, v in ins.items()})
    ref = np.asarray(R.reference(**{k: np.asarray(v) for k, v in ins.items()}))
    rv = float(np.mean((out - ref) ** 2) / np.mean(ref ** 2))
    print("maxabs", np.abs(out - ref).max(), "resvar", rv)


# version 09
# speedup vs baseline: 1.3810x; 1.3810x over previous
import numpy as np
import ml_dtypes
from contextlib import ExitStack

import concourse.bass as bass
import concourse.mybir as mybir
from concourse.bass_utils import run_bass_kernel_spmd

F32 = mybir.dt.float32
BF16 = mybir.dt.bfloat16
AF = mybir.ActivationFunctionType
ALU = mybir.AluOpType
AX = mybir.AxisListType

T = 2048
C = 1024
NB = 2
NCH = 4
CHK = 512
NP = 8
EPS = 1e-5
ENGS = ("tensor", "vector", "scalar", "gpsimd", "sync")
DEBUG = False
SKIP_ATTN = False
TAPS = {}


def build_program():
    nc = bass.Bass("TRN2", target_bir_lowering=False)

    d_x = nc.dram_tensor("xT", [NB, C, T], F32, kind="ExternalInput").ap()
    d_out = nc.dram_tensor("outT", [NB, C, T], F32, kind="ExternalOutput").ap()
    d_wr = nc.dram_tensor("wrT", [C, C], F32, kind="ExternalInput").ap()
    d_wk = nc.dram_tensor("wkT", [C, C], F32, kind="ExternalInput").ap()
    d_wv = nc.dram_tensor("wvT", [C, C], F32, kind="ExternalInput").ap()
    d_wq = nc.dram_tensor("wqT", [C, 512], F32, kind="ExternalInput").ap()
    d_wqk = nc.dram_tensor("wqkT", [512, C], F32, kind="ExternalInput").ap()
    d_wkvr = nc.dram_tensor("wkvrT", [C, 128], F32, kind="ExternalInput").ap()
    d_wvu = nc.dram_tensor("wvuT", [64, C], F32, kind="ExternalInput").ap()
    d_wpj = nc.dram_tensor("wpjT", [C, C], F32, kind="ExternalInput").ap()
    d_mur = nc.dram_tensor("mur", [128, 8], F32, kind="ExternalInput").ap()
    d_muk = nc.dram_tensor("muk", [128, 8], F32, kind="ExternalInput").ap()
    d_bq = nc.dram_tensor("bq", [128, 4], F32, kind="ExternalInput").ap()
    d_bqk = nc.dram_tensor("bqk", [128, 8], F32, kind="ExternalInput").ap()
    d_bkvr = nc.dram_tensor("bkvr", [128, 1], F32, kind="ExternalInput").ap()
    d_bvu = nc.dram_tensor("bvu", [128, 8], F32, kind="ExternalInput").ap()
    d_bpj = nc.dram_tensor("bpj", [128, 8], F32, kind="ExternalInput").ap()
    d_idn = nc.dram_tensor("idn", [128, 128], BF16, kind="ExternalInput").ap()
    taps = {}
    if DEBUG:
        for nm, shp, dt in (("tapA", [C, T], F32), ("tapB", [C, T], F32),
                            ("tapCD", [C, T], F32), ("tapckv", [128, T], BF16),
                            ("tapq", [128, 8 * 1024], BF16),
                            ("tapyh", [128, 8 * 1024], BF16),
                            ("tapnegm", [128, 16], F32),
                            ("tapcq", [128, 4 * T], BF16),
                            ("tapxr", [128, 8 * 512], BF16),
                            ("tapxk", [128, 8 * 512], BF16),
                            ("tapr0", [128, 8 * 512], BF16),
                            ("tapka0", [128, 8 * 512], BF16),
                            ("tapxh0", [128, 8 * 513], F32),
                            ("tapx0", [128, 8 * 512], F32),
                            ("tapx1", [128, 8 * 512], F32),
                            ("tapw", [128, 8 * 1024], BF16),
                            ("tapyh0", [128, 8 * 1024], BF16)):
            taps[nm] = nc.dram_tensor(nm, shp, dt, kind="ExternalOutput").ap()

    streams = {k: [] for k in ENGS}

    with ExitStack() as ctx:
        sb = lambda n, s, d: ctx.enter_context(nc.sbuf_tensor(n, s, d))

        x_sb = sb("x_sb", [128, 8 * T], F32)
        wr_sb = sb("wr_sb", [128, 8 * 1024], BF16)
        wk_sb = sb("wk_sb", [128, 8 * 1024], BF16)
        wvs_sb = sb("wvs_sb", [128, 8 * 1024], BF16)
        wq_sb = sb("wq_sb", [128, 8 * 512], BF16)
        wqk_sb = sb("wqk_sb", [128, 4 * 1024], BF16)
        wkvr_sb = sb("wkvr_sb", [128, 8 * 128], BF16)
        wvu_sb = sb("wvu_sb", [64, 1024], BF16)
        arena = sb("arena", [128, 37392], BF16)

        mur_sb = sb("mur_sb", [128, 8], F32)
        muk_sb = sb("muk_sb", [128, 8], F32)
        bq_sb = sb("bq_sb", [128, 4], F32)
        bqk_sb = sb("bqk_sb", [128, 8], F32)
        bkvr_sb = sb("bkvr_sb", [128, 1], F32)
        bvu_bf = sb("bvu_bf", [128, 8], BF16)
        bpj_sb = sb("bpj_sb", [128, 8], F32)
        beff_sb = sb("beff_sb", [128, 8], F32)
        eps_sb = sb("eps_sb", [128, 1], F32)
        idn_sb = sb("idn_sb", [128, 128], BF16)
        ones1 = sb("ones1", [128, 128], BF16)
        onesm = sb("onesm", [128, 128], BF16)
        negm_sb = sb("negm_sb", [128, 16], F32)
        pmax2 = sb("pmax2", [128, 2], BF16)
        m2b = sb("m2b", [128, 1], BF16)
        rowb = sb("rowb", [128, 2], BF16)

        PSA = ctx.enter_context(nc.psum_tensor("PSA", [128, 1024], F32))
        PSB = ctx.enter_context(nc.psum_tensor("PSB", [128, 1024], F32))
        PSY = ctx.enter_context(nc.psum_tensor("PSY", [128, 1024], F32))
        PSM = ctx.enter_context(nc.psum_tensor("PSM", [128, 512], F32))
        PSBF = ctx.enter_context(nc.psum_tensor("PSBF", [128, 256], BF16))

        SLOT = [PSA[:, 0:512], PSA[:, 512:1024], PSB[:, 0:512],
                PSB[:, 512:1024], PSY[:, 0:512]]
        SLOT_RG = ["A0", "A1", "B0", "B1", "Y0"]
        MSQ = PSY[:, 512:1024]

        arb = arena[:]
        arf = arena[:].bitcast(F32)

        def vb(off, shape):
            n = int(np.prod(shape[1:]))
            a = arb[:, off // 2: off // 2 + n]
            if len(shape) == 3:
                a = a.rearrange("p (a b) -> p a b", a=shape[1])
            elif len(shape) == 4:
                a = a.rearrange("p (a b c) -> p a b c", a=shape[1], b=shape[2])
            return a

        def vf(off, shape):
            n = int(np.prod(shape[1:]))
            a = arf[:, off // 4: off // 4 + n]
            if len(shape) == 3:
                a = a.rearrange("p (a b) -> p a b", a=shape[1])
            return a

        xr_bf = vb(0, [128, 8, 512])
        xbf2 = vb(8192, [128, 8, 512])
        xsq = vb(16384, [128, 8, 512])
        gpd = vf(16384, [128, 512])
        mean_sb = vf(24576, [128, 512])
        var_sb = vf(26624, [128, 512])
        tmpv = vf(28672, [128, 512])
        qhalf = vb(0, [128, 8, 1024])
        vaug = vb(16384, [128, 16, 256])
        vaug4 = vb(16384, [128, 16, 4, 64])
        expA = vb(24576, [128, 2, 1024])
        xhat = vf(30720, [128, 8, 513])
        xk_bf = vb(47136, [128, 8, 512])
        kact = vb(55328, [128, 8, 512])
        r_bf = vb(63520, [128, 8, 512])
        dbuf = vf(71712, [128, 512])
        relu2 = vb(73760, [128, 512])
        cq_bf = vb(30720, [128, 4, 2048])
        ckv = vb(47104, [128, 2048])
        yhat = vb(51200, [128, 8, 1024])
        expB = vb(67584, [128, 2, 1024])
        recmv = vb(71680, [128, 512])

        x3 = x_sb[:].rearrange("p (ct t) -> p ct t", ct=8)
        wrv = wr_sb[:].rearrange("p (k d) -> p k d", k=8)
        wkv_ = wk_sb[:].rearrange("p (k d) -> p k d", k=8)
        wvsv = wvs_sb[:].rearrange("p (k d) -> p k d", k=8)
        wqv = wq_sb[:].rearrange("p (k d) -> p k d", k=8)
        wqkv = wqk_sb[:].rearrange("p (k d) -> p k d", k=4)
        wkvrv = wkvr_sb[:].rearrange("p (k d) -> p k d", k=8)

        class Sem:
            def __init__(self, name):
                self.h = ctx.enter_context(nc.semaphore(name))
                self.n = 0

        ST, SS, SV, SG = Sem("ST"), Sem("SS"), Sem("SV"), Sem("SG")
        SDW, SDX, SO = Sem("SDW"), Sem("SDX"), Sem("SO")

        def op(eng, build, sem=None, k=1):
            if sem is None:
                streams[eng].append(build)
                return None
            sem.n += k
            streams[eng].append(
                lambda e, b=build, hh=sem.h, kv=k: b(e).then_inc(hh, kv))
            return sem.n

        def wt(eng, sem, val):
            if sem is None or val is None or val <= 0:
                return
            streams[eng].append(lambda e, hh=sem.h, v=val: e.wait_ge(hh, v))

        FREE = {r: (SV, 0) for r in
                ("A0", "A1", "B0", "B1", "Y0", "Y1", "M", "BF")}

        def waitfree(eng, *regions):
            for r in regions:
                s, v = FREE[r]
                wt(eng, s, v)

        def setfree(region, sem, val):
            FREE[region] = (sem, val)

        xin_ch = [[0] * NCH for _ in range(NB)]

        def emit_xload(b, ch):
            src = d_x[b].rearrange("(ct p) t -> p ct t", p=128)[
                :, :, ch * CHK:(ch + 1) * CHK]
            dst = x3[:, :, ch * CHK:(ch + 1) * CHK]
            xin_ch[b][ch] = op("sync",
                               lambda e, s=src, d=dst: e.dma_start(out=d, in_=s),
                               SDX, 16)

        for ch in range(NCH):
            emit_xload(0, ch)
        for dst, src in ((mur_sb, d_mur), (muk_sb, d_muk), (bq_sb, d_bq),
                         (bqk_sb, d_bqk), (bkvr_sb, d_bkvr), (bpj_sb, d_bpj),
                         (idn_sb, d_idn)):
            op("sync", lambda e, d=dst, s=src: e.dma_start(out=d[:], in_=s),
               SDX, 16)
        sdx_smalls = SDX.n

        wdma = {}

        def emit_wload(name, dst, src):
            wdma[name] = op("gpsimd",
                            lambda e, d=dst, s=src: e.dma_start(out=d, in_=s),
                            SDW, 16)

        emit_wload("wr", wrv, d_wr.rearrange("(k p) d -> p k d", p=128))
        emit_wload("wk", wkv_, d_wk.rearrange("(k p) d -> p k d", p=128))
        emit_wload("wv0", wvsv, d_wv.rearrange("(k p) d -> p k d", p=128))
        emit_wload("wkvr", wkvrv, d_wkvr.rearrange("(k p) d -> p k d", p=128))
        emit_wload("wq", wqv, d_wq.rearrange("(k p) d -> p k d", p=128))
        emit_wload("wqk", wqkv, d_wqk.rearrange("(k p) d -> p k d", p=128))
        emit_wload("wvu", wvu_sb[:], d_wvu)
        emit_wload("bvu", bvu_bf[:], d_bvu)

        op("vector", lambda e: e.memset(ones1[:], 1.0))
        op("vector", lambda e: e.memset(onesm[:], 1.0 / 1024.0))
        op("vector", lambda e: e.memset(eps_sb[:], EPS))
        op("vector", lambda e: e.memset(m2b[:], 0.0))
        op("vector", lambda e: e.memset(pmax2[:], 0.0))
        v_ones = op("vector", lambda e: e.memset(m2b[:], 0.0), SV)
        for _eng in ("tensor", "scalar", "gpsimd"):
            wt(_eng, SV, v_ones)
            wt(_eng, SDX, sdx_smalls)
        wt("vector", SDX, sdx_smalls)

        stats_gate = [0, 0]
        x_upd = {}
        wv_reload = [0]
        si = [0]
        slot_free = [(SV, 0)] * 5
        probe_prev = {"t1": 0, "t2": 0, "t3": 0}
        negm_gate = [0]
        xsq_gate = [0]
        psm_recmv_cons = [0]
        last_tmv = [0]
        kvev = [0] * 8

        def stats_stage(b, ch, xgate):
            cc = slice(ch * CHK, (ch + 1) * CHK)
            wt("gpsimd", SDX, xin_ch[b][ch])
            if xgate:
                wt("gpsimd", xgate[0], xgate[1])
            wt("gpsimd", ST, stats_gate[0])
            g_cast = op("gpsimd", lambda e, c=cc: e.tensor_copy(
                out=xbf2[:, :, :], in_=x3[:, :, c]), SG)
            wt("scalar", SDX, xin_ch[b][ch])
            if xgate:
                wt("scalar", xgate[0], xgate[1])
            wt("scalar", ST, stats_gate[1])
            wt("scalar", SV, xsq_gate[0])
            s_xsq = op("scalar", lambda e, c=cc: e.activation(
                out=xsq[:, :, :], in_=x3[:, :, c], func=AF.Square), SS)

            wt("tensor", SG, g_cast)
            wt("tensor", SS, s_xsq)
            waitfree("tensor", "M", "Y1")
            t_mean = None
            for k in range(8):
                t_mean = op("tensor", lambda e, kk=k: e.matmul(
                    PSM[:], onesm[:], xbf2[:, kk, :],
                    start=(kk == 0), stop=(kk == 7)), ST if k == 7 else None)
            t_msq = None
            for k in range(8):
                t_msq = op("tensor", lambda e, kk=k: e.matmul(
                    MSQ, onesm[:], xsq[:, kk, :],
                    start=(kk == 0), stop=(kk == 7)), ST if k == 7 else None)
            stats_gate[0] = t_mean
            stats_gate[1] = t_msq

            wt("scalar", ST, t_mean)
            s_mean = op("scalar", lambda e: e.activation(
                out=mean_sb[:], in_=PSM[:], func=AF.Identity), SS)
            setfree("M", SS, s_mean)
            s_m2 = op("scalar", lambda e: e.activation(
                out=var_sb[:], in_=mean_sb[:], func=AF.Square), SS)
            wt("vector", ST, t_msq)
            wt("vector", SS, s_m2)
            v_var = op("vector", lambda e: e.tensor_tensor(
                out=var_sb[:], in0=MSQ, in1=var_sb[:], op=ALU.subtract), SV)
            setfree("Y1", SV, v_var)
            wt("scalar", SV, v_var)
            op("scalar", lambda e: e.activation(
                out=var_sb[:], in_=var_sb[:], func=AF.Ln,
                bias=eps_sb[:, 0:1], scale=1.0))
            s_rstd = op("scalar", lambda e: e.activation(
                out=var_sb[:], in_=var_sb[:], func=AF.Exp,
                bias=0.0, scale=-0.5), SS)
            return dict(rstd=s_rstd, cc=cc)

        VCT, GCT = (0, 1, 2, 3), (4, 5, 6, 7)
        gpd_ct = {c: vf(16384 + 2048 * i, [128, 512]) for i, c in enumerate(GCT)}

        def mix_stage(b, ch, st, r_gate, k_gate):
            cc = st["cc"]
            wt("gpsimd", SS, st["rstd"])
            wt("gpsimd", ST, max(r_gate or 0, k_gate or 0))
            wt("gpsimd", SV, xsq_gate[0])
            g_d = {}
            for ct in GCT:
                if ch == 0:
                    op("gpsimd", lambda e, c=ct: e.memset(xhat[:, c, 0:1], 0.0))
                else:
                    op("gpsimd", lambda e, c=ct: e.tensor_copy(
                        out=xhat[:, c, 0:1], in_=xhat[:, c, 512:513]))
                op("gpsimd", lambda e, c=ct, s=cc: e.tensor_tensor(
                    out=xhat[:, c, 1:513], in0=x3[:, c, s], in1=mean_sb[:],
                    op=ALU.subtract))
                op("gpsimd", lambda e, c=ct: e.tensor_tensor(
                    out=xhat[:, c, 1:513], in0=xhat[:, c, 1:513],
                    in1=var_sb[:], op=ALU.mult))
                g_d[ct] = op("gpsimd", lambda e, c=ct: e.tensor_tensor(
                    out=gpd_ct[c][:], in0=xhat[:, c, 1:513],
                    in1=xhat[:, c, 0:512], op=ALU.subtract), SG)
            wt("vector", SS, st["rstd"])
            wt("vector", ST, r_gate)
            wt("vector", ST, k_gate)
            v_last = None
            for ct in VCT:
                if ch == 0:
                    op("vector", lambda e, c=ct: e.memset(xhat[:, c, 0:1], 0.0))
                else:
                    op("vector", lambda e, c=ct: e.tensor_copy(
                        out=xhat[:, c, 0:1], in_=xhat[:, c, 512:513]))
                op("vector", lambda e, c=ct, s=cc: e.tensor_tensor(
                    out=xhat[:, c, 1:513], in0=x3[:, c, s], in1=mean_sb[:],
                    op=ALU.subtract))
                op("vector", lambda e, c=ct: e.tensor_tensor(
                    out=xhat[:, c, 1:513], in0=xhat[:, c, 1:513],
                    in1=var_sb[:], op=ALU.mult))
                op("vector", lambda e, c=ct: e.tensor_tensor(
                    out=dbuf[:], in0=xhat[:, c, 1:513], in1=xhat[:, c, 0:512],
                    op=ALU.subtract))
                op("vector", lambda e, c=ct: e.scalar_tensor_tensor(
                    out=xr_bf[:, c, :], in0=dbuf[:], scalar=mur_sb[:, c:c + 1],
                    in1=xhat[:, c, 0:512], op0=ALU.mult, op1=ALU.add))
                v_last = op("vector", lambda e, c=ct: e.scalar_tensor_tensor(
                    out=xk_bf[:, c, :], in0=dbuf[:], scalar=muk_sb[:, c:c + 1],
                    in1=xhat[:, c, 0:512], op0=ALU.mult, op1=ALU.add), SV)
            for ct in GCT:
                wt("vector", SG, g_d[ct])
                op("vector", lambda e, c=ct: e.scalar_tensor_tensor(
                    out=xr_bf[:, c, :], in0=gpd_ct[c][:],
                    scalar=mur_sb[:, c:c + 1], in1=xhat[:, c, 0:512],
                    op0=ALU.mult, op1=ALU.add))
                v_last = op("vector", lambda e, c=ct: e.scalar_tensor_tensor(
                    out=xk_bf[:, c, :], in0=gpd_ct[c][:],
                    scalar=muk_sb[:, c:c + 1], in1=xhat[:, c, 0:512],
                    op0=ALU.mult, op1=ALU.add), SV)
            xsq_gate[0] = v_last
            return {"vector": v_last, "gpsimd": g_d[GCT[-1]]}

        def mm_group(w_view, rhs_view, m, wgate=None):
            s = si[0] % 5
            si[0] += 1
            sl = SLOT[s]
            waitfree("tensor", SLOT_RG[s])
            if wgate:
                wt("tensor", SDW, wgate)
            t = None
            for k in range(8):
                t = op("tensor", lambda e, kk=k, sl_=sl, w=w_view, r=rhs_view,
                       mm=m: e.matmul(sl_, w[:, kk, mm * 128:(mm + 1) * 128],
                                      r[:, kk, :], start=(kk == 0),
                                      stop=(kk == 7)), ST if k == 7 else None)
            return s, sl, t

        def ffn_phase(b, is_E):
            ph = "E" if is_E else "A"
            gate_all = ST.n
            stats_gate[0] = max(stats_gate[0], gate_all)
            stats_gate[1] = max(stats_gate[1], gate_all)

            def xgate(ch):
                if is_E:
                    if SKIP_ATTN:
                        return x_upd.get((b, "A", ch))
                    return x_upd.get((b, "D", ch))
                return None

            st = stats_stage(b, 0, xgate(0))
            mix = mix_stage(b, 0, st, gate_all, gate_all)
            r_last = [gate_all]
            k_last = [gate_all]
            for ch in range(NCH):
                cc = slice(ch * CHK, (ch + 1) * CHK)
                wt("tensor", SV, mix["vector"])
                wt("tensor", SG, mix["gpsimd"])
                t_r = None
                for m in range(8):
                    s, sl, t = mm_group(wrv, xr_bf, m,
                                        wdma["wr"] if (b == 0 and not is_E and
                                                       ch == 0) else None)
                    t_r = t
                    wt("scalar", ST, t)
                    wt("scalar", SV, kvev[m])
                    c = op("scalar", lambda e, sl_=sl, mm=m: e.activation(
                        out=r_bf[:, mm, :], in_=sl_, func=AF.Sigmoid), SS)
                    setfree(SLOT_RG[s], SS, c)
                r_last[0] = t_r
                if ch + 1 < NCH:
                    st2 = stats_stage(b, ch + 1, xgate(ch + 1))
                t_k = None
                for m in range(8):
                    s, sl, t = mm_group(wkv_, xk_bf, m,
                                        wdma["wk"] if (b == 0 and not is_E and
                                                       ch == 0) else None)
                    t_k = t
                    wt("scalar", ST, t)
                    c = op("scalar", lambda e, sl_=sl, mm=m: e.activation(
                        out=kact[:, mm, :], in_=sl_, func=AF.Relu), SS)
                    wt("vector", SS, c)
                    cv = op("vector", lambda e, sl_=sl, mm=m:
                            e.tensor_tensor(out=kact[:, mm, :],
                                            in0=kact[:, mm, :],
                                            in1=sl_, op=ALU.mult), SV)
                    setfree(SLOT_RG[s], SV, cv)
                k_last[0] = t_k
                xdone = None
                for m in range(8):
                    wg = None
                    if is_E and ch == 0 and m == 0:
                        wg = wv_reload[0]
                    elif b == 0 and not is_E and ch == 0:
                        wg = wdma["wv0"]
                    s, sl, t = mm_group(wvsv, kact, m, wg)
                    wt("vector", ST, t)
                    op("vector", lambda e, sl_=sl, mm=m: e.scalar_tensor_tensor(
                        out=tmpv[:], in0=sl_, scalar=0.5, in1=r_bf[:, mm, :],
                        op0=ALU.mult, op1=ALU.mult))
                    cv = op("vector", lambda e, mm=m, c2=cc: e.tensor_tensor(
                        out=x3[:, mm, c2], in0=x3[:, mm, c2], in1=tmpv[:],
                        op=ALU.add), SV)
                    setfree(SLOT_RG[s], SV, cv)
                    kvev[m] = cv
                    xdone = cv
                x_upd[(b, ph, ch)] = (SV, xdone)
                if DEBUG and b == 0 and not is_E and ch == 0:
                    wt("sync", SV, xdone)
                    op("sync", lambda e: e.dma_start(out=taps["tapr0"], in_=r_bf), SO, 16)
                    op("sync", lambda e: e.dma_start(out=taps["tapka0"], in_=kact), SO, 16)
                    op("sync", lambda e: e.dma_start(out=taps["tapxh0"], in_=xhat), SO, 16)
                    op("sync", lambda e: e.dma_start(out=taps["tapx0"], in_=x3[:, :, 0:512]), SO, 16)
                if DEBUG and b == 0 and not is_E and ch == 1:
                    wt("sync", SV, xdone)
                    op("sync", lambda e: e.dma_start(out=taps["tapx1"], in_=x3[:, :, 512:1024]), SO, 16)
                if is_E:
                    wt("sync", SV, xdone)
                    so = op("sync", lambda e, bb=b, c2=cc, c=ch: e.dma_start(
                        out=d_out[bb].rearrange("(ct p) t -> p ct t", p=128)[
                            :, :, c * CHK:(c + 1) * CHK],
                        in_=x3[:, :, c2]), SO, 16)
                    if b == 0:
                        wt("sync", SO, so)
                        emit_xload(1, ch)
                if ch + 1 < NCH:
                    mix = mix_stage(b, ch + 1, st2, r_last[0], k_last[0])
                    st = st2

        def phase_b(b):
            wt("gpsimd", ST, ST.n)
            wdma["wpj"] = op("gpsimd", lambda e: e.dma_start(
                out=wvsv, in_=d_wpj.rearrange("(k p) d -> p k d", p=128)),
                SDW, 16)
            prev_ck = 0
            for ch in range(NCH):
                cc = slice(ch * CHK, (ch + 1) * CHK)
                st = stats_stage(b, ch, x_upd[(b, "A", ch)])
                wt("vector", SS, st["rstd"])
                wt("vector", ST, prev_ck)
                xhb = None
                for ct in range(8):
                    op("vector", lambda e, c=ct, s=cc: e.tensor_tensor(
                        out=tmpv[:], in0=x3[:, c, s], in1=mean_sb[:],
                        op=ALU.subtract))
                    xhb = op("vector", lambda e, c=ct: e.tensor_tensor(
                        out=xr_bf[:, c, :], in0=tmpv[:], in1=var_sb[:],
                        op=ALU.mult), SV)
                wt("tensor", SV, xhb)
                for mq in range(4):
                    waitfree("tensor", SLOT_RG[mq])
                    if b == 0 and ch == 0 and mq == 0:
                        wt("tensor", SDW, wdma["wq"])
                    t = None
                    for k in range(8):
                        t = op("tensor", lambda e, kk=k, m=mq: e.matmul(
                            SLOT[m], wqv[:, kk, m * 128:(m + 1) * 128],
                            xr_bf[:, kk, :], start=(kk == 0), stop=(kk == 7)),
                            ST if k == 7 else None)
                    wt("scalar", ST, t)
                    c = op("scalar", lambda e, m=mq, s=cc: e.activation(
                        out=cq_bf[:, m, s], in_=SLOT[m], func=AF.Identity,
                        bias=bq_sb[:, m:m + 1], scale=1.0), SS)
                    setfree(SLOT_RG[mq], SS, c)
                waitfree("tensor", "Y0")
                if b == 0 and ch == 0:
                    wt("tensor", SDW, wdma["wkvr"])
                t = None
                for k in range(8):
                    t = op("tensor", lambda e, kk=k: e.matmul(
                        SLOT[4], wkvrv[:, kk, :], xr_bf[:, kk, :],
                        start=(kk == 0), stop=(kk == 7)), ST if k == 7 else None)
                prev_ck = t
                wt("scalar", ST, t)
                c = op("scalar", lambda e, s=cc: e.activation(
                    out=ckv[:, s], in_=SLOT[4], func=AF.Identity,
                    bias=bkvr_sb[:, 0:1], scale=1.0), SS)
                setfree("Y0", SS, c)
            if b == 0:
                wt("tensor", SDW, wdma["wpj"])
                wt("tensor", SDW, wdma["bvu"])
                waitfree("tensor", "M")
                t = None
                for m in range(8):
                    for k in range(8):
                        t = op("tensor", lambda e, kk=k, mm=m: e.matmul(
                            PSM[:, mm:mm + 1],
                            wvsv[:, kk, mm * 128:(mm + 1) * 128],
                            bvu_bf[:, kk:kk + 1], start=(kk == 0),
                            stop=(kk == 7)),
                            ST if (k == 7 and m == 7) else None)
                wt("vector", ST, t)
                v = op("vector", lambda e: e.tensor_tensor(
                    out=beff_sb[:], in0=PSM[:, 0:8], in1=bpj_sb[:],
                    op=ALU.add), SV)
                setfree("M", SV, v)
                return v
            return None

        def probe_chain(p, half):
            waitfree("tensor", "M")
            wt("tensor", ST, probe_prev["t1"])
            tA = op("tensor", lambda e, pp=p, h=half: e.matmul(
                PSM[:], ckv[0:64, 0:128],
                qhalf[0:64, pp, 0:512],
                start=True, stop=True, tile_position=(0, 0)), ST)
            wt("vector", ST, tA)
            vA = op("vector", lambda e: e.reduce_max(
                out=pmax2[:, 0:1], in_=PSM[:], axis=AX.X), SV)
            wt("tensor", SV, vA)
            tB = op("tensor", lambda e, pp=p: e.matmul(
                PSM[:], ckv[64:128, 0:128], qhalf[64:128, pp, 0:512],
                start=True, stop=True, tile_position=(64, 0)), ST)
            wt("vector", ST, tB)
            vB = op("vector", lambda e: e.reduce_max(
                out=pmax2[:, 1:2], in_=PSM[:], axis=AX.X), SV)
            waitfree("tensor", "BF")
            wt("tensor", SV, vB)
            t1 = op("tensor", lambda e: e.transpose(
                out=PSBF[0:2, 0:128], in_=pmax2[:], identity=idn_sb[:]), ST)
            probe_prev["t1"] = t1
            wt("vector", ST, t1)
            wt("vector", ST, probe_prev["t2"])
            v2 = op("vector", lambda e: e.reduce_max(
                out=m2b[0:2, 0:1], in_=PSBF[0:2, 0:128], axis=AX.X), SV)
            wt("tensor", SV, v2)
            t2 = op("tensor", lambda e: e.transpose(
                out=PSBF[0:1, 128:256], in_=m2b[:], identity=idn_sb[:]), ST)
            probe_prev["t2"] = t2
            wt("scalar", ST, t2)
            wt("scalar", ST, probe_prev["t3"])
            s1 = op("scalar", lambda e: e.activation(
                out=rowb[0:1, 0:2], in_=PSBF[0:1, 128:130],
                func=AF.Identity), SS)
            setfree("BF", SS, s1)
            wt("tensor", SS, s1)
            t3 = op("tensor", lambda e: e.matmul(
                PSM[:, 0:2], ones1[0:1, :], rowb[0:1, 0:2],
                start=True, stop=True), ST)
            probe_prev["t3"] = t3
            wt("vector", ST, t3)
            wt("vector", SS, negm_gate[0])
            v3 = op("vector", lambda e, pp=p: e.tensor_scalar(
                out=negm_sb[:, 2 * pp:2 * pp + 2], in0=PSM[:, 0:2],
                scalar1=-0.125, scalar2=None, op0=ALU.mult), SV)
            setfree("M", SV, v3)
            return v3

        def phase_cd(b, beff_done):
            negm_ready = [0] * NP
            negm_gate[0] = SS.n
            cq_ready = SS.n
            for half in range(2):
                hoff = half * 1024
                if b == 0 and half == 0:
                    wt("tensor", SDW, wdma["wqk"])
                wt("tensor", SS, cq_ready)
                qdone = None
                for p in range(NP):
                    sidx = p % 4
                    for tb2 in range(2):
                        waitfree("tensor", SLOT_RG[sidx])
                        t = None
                        for k in range(4):
                            t = op("tensor", lambda e, kk=k, pp=p, t2=tb2, h2=hoff:
                                   e.matmul(SLOT[pp % 4],
                                            wqkv[:, kk, pp * 128:(pp + 1) * 128],
                                            cq_bf[:, kk, h2 + t2 * 512:
                                                  h2 + (t2 + 1) * 512],
                                            start=(kk == 0), stop=(kk == 3)),
                                   ST if k == 3 else None)
                        wt("scalar", ST, t)
                        qdone = op("scalar", lambda e, pp=p, t2=tb2: e.activation(
                            out=qhalf[:, pp, t2 * 512:(t2 + 1) * 512],
                            in_=SLOT[pp % 4], func=AF.Identity,
                            bias=bqk_sb[:, pp:pp + 1], scale=1.0), SS)
                        setfree(SLOT_RG[sidx], SS, qdone)
                if half == 0:
                    wt("tensor", SS, qdone)
                    negm_ready[0] = probe_chain(0, half)
                expA_free = [(ST, 0), (ST, 0)]
                expB_free = [(ST, 0), (ST, 0)]
                for p in range(NP):
                    vevac = None
                    for rnd in range(2):
                        waitfree("tensor", "Y0", "Y1")
                        if rnd == 0:
                            wt("tensor", SV, v_ones)
                            if b == 0 and half == 0 and p == 0:
                                wt("tensor", SDW, wdma["wvu"])
                        else:
                            wt("tensor", SV, vevac)
                        t = None
                        for s8 in range(8):
                            stile = rnd * 8 + s8
                            t = op("tensor", lambda e, s_=s8, sl_=stile, pp=p:
                                   e.matmul(PSY[:, s_ * 128:(s_ + 1) * 128],
                                            ckv[0:64, sl_ * 128:(sl_ + 1) * 128],
                                            wvu_sb[0:64, pp * 128:(pp + 1) * 128],
                                            start=True, stop=True),
                                   ST if s8 == 7 else None)
                        wt("vector", ST, t)
                        psyv = PSY[:].rearrange("p (s g c) -> p s g c", s=8, g=2)
                        op("vector", lambda e, r=rnd: e.memset(
                            vaug4[:, r * 8:(r + 1) * 8, 1:3, :], 1.0))
                        op("vector", lambda e, r=rnd, pv=psyv: e.tensor_copy(
                            out=vaug4[:, r * 8:(r + 1) * 8, 0, :],
                            in_=pv[:, :, 0, :]))
                        vevac = op("vector", lambda e, r=rnd, pv=psyv:
                                   e.tensor_copy(
                                       out=vaug4[:, r * 8:(r + 1) * 8, 3, :],
                                       in_=pv[:, :, 1, :]), SV)
                    vj_done = vevac
                    setfree("Y0", SV, vj_done)
                    setfree("Y1", SV, vj_done)
                    if half == 0 and p + 1 < NP:
                        negm_ready[p + 1] = probe_chain(p + 1, half)
                    units = [(tb, jp) for tb in range(2) for jp in range(8)]
                    pend = None

                    def emit_attnv(u):
                        tbp, jpp = u
                        parp = (tbp * 8 + jpp) % 2
                        if jpp == 0:
                            wt("tensor", SV, vj_done)
                            waitfree("tensor", "Y0", "Y1")
                        wt("tensor", SS, eAd[parp])
                        aA = None
                        for odd in range(2):
                            stile = 2 * jpp + odd
                            aA = op("tensor", lambda e, s_=stile, o=odd,
                                    pa=parp: e.matmul(
                                        PSY[:, 0:512], vaug[:, s_, 0:128],
                                        expA[:, pa, o * 512:(o + 1) * 512],
                                        start=(s_ == 0), stop=(s_ == 15)),
                                    ST if odd == 1 else None)
                        expA_free[parp] = (ST, aA)
                        wt("tensor", SS, eBd[parp])
                        al = None
                        for odd in range(2):
                            stile = 2 * jpp + odd
                            al = op("tensor", lambda e, s_=stile, o=odd,
                                    pa=parp: e.matmul(
                                        PSY[:, 512:1024], vaug[:, s_, 128:256],
                                        expB[:, pa, o * 512:(o + 1) * 512],
                                        start=(s_ == 0), stop=(s_ == 15)),
                                    ST if odd == 1 else None)
                        expB_free[parp] = (ST, al)
                        return al

                    def emit_norm(tbp, att_last):
                        t0 = tbp * 512
                        wt("scalar", ST, att_last)
                        op("scalar", lambda e: e.activation(
                            out=tmpv[64:128, :], in_=PSY[64:128, 0:512],
                            func=AF.Ln))
                        op("scalar", lambda e: e.activation(
                            out=tmpv[0:64, :], in_=PSY[0:64, 512:1024],
                            func=AF.Ln))
                        wt("scalar", ST, last_tmv[0])
                        wt("scalar", SV, psm_recmv_cons[0])
                        s_rec = op("scalar", lambda e: e.activation(
                            out=recmv[:], in_=tmpv[:], func=AF.Exp,
                            bias=0.0, scale=-1.0), SS)
                        waitfree("tensor", "M")
                        wt("tensor", SS, s_rec)
                        op("tensor", lambda e: e.matmul(
                            PSM[0:64, :], idn_sb[64:128, 64:128],
                            recmv[64:128, :], start=True, stop=True,
                            tile_position=(64, 0)))
                        t_mv = op("tensor", lambda e: e.matmul(
                            PSM[64:128, :], idn_sb[0:64, 0:64], recmv[0:64, :],
                            start=True, stop=True, tile_position=(0, 64)), ST)
                        last_tmv[0] = t_mv
                        wt("vector", ST, t_mv)
                        v_rc = op("vector", lambda e: e.tensor_copy(
                            out=recmv[:], in_=PSM[:]), SV)
                        setfree("M", SV, v_rc)
                        op("vector", lambda e, pp=p, tt=t0: e.tensor_tensor(
                            out=yhat[0:64, pp, tt:tt + 512],
                            in0=PSY[0:64, 0:512], in1=recmv[0:64, :],
                            op=ALU.mult))
                        v_yh = op("vector", lambda e, pp=p, tt=t0:
                                  e.tensor_tensor(
                                      out=yhat[64:128, pp, tt:tt + 512],
                                      in0=PSY[64:128, 512:1024],
                                      in1=recmv[64:128, :], op=ALU.mult), SV)
                        psm_recmv_cons[0] = v_yh
                        setfree("Y0", SV, v_yh)
                        setfree("Y1", SV, v_yh)

                    eAd = [0, 0]
                    eBd = [0, 0]
                    for i, (tb, jp) in enumerate(units):
                        t0 = tb * 512
                        par = i % 2
                        waitfree("tensor", "A0", "A1")
                        tscA = None
                        for odd in range(2):
                            stile = 2 * jp + odd
                            tscA = op("tensor", lambda e, s_=stile, o=odd,
                                      pp=p, tt=t0, c_=par * 0: e.matmul(
                                          PSA[:, (o + 0) * 512:(o + 1) * 512]
                                          if False else
                                          PSA[:, o * 512:(o + 1) * 512],
                                          ckv[0:64, s_ * 128:(s_ + 1) * 128],
                                          qhalf[0:64, pp, tt:tt + 512],
                                          start=True, stop=True,
                                          tile_position=(0, 0)),
                                      ST if odd == 1 else None)
                        waitfree("tensor", "B0", "B1")
                        tscB = None
                        for odd in range(2):
                            stile = 2 * jp + odd
                            tscB = op("tensor", lambda e, s_=stile, o=odd,
                                      pp=p, tt=t0: e.matmul(
                                          PSB[:, o * 512:(o + 1) * 512],
                                          ckv[64:128, s_ * 128:(s_ + 1) * 128],
                                          qhalf[64:128, pp, tt:tt + 512],
                                          start=True, stop=True,
                                          tile_position=(64, 0)),
                                      ST if odd == 1 else None)
                        if pend is not None:
                            al = emit_attnv(pend)
                            if pend[1] == 7:
                                emit_norm(pend[0], al)
                        wt("scalar", ST, tscA)
                        wt("scalar", SV, negm_ready[p])
                        wt("scalar", expA_free[par][0], expA_free[par][1])
                        eA = op("scalar", lambda e, pa=par, pp=p:
                                e.activation(out=expA[:, pa, :], in_=PSA[:],
                                             func=AF.Exp,
                                             bias=negm_sb[:, 2 * pp:2 * pp + 1],
                                             scale=0.125), SS)
                        setfree("A0", SS, eA)
                        setfree("A1", SS, eA)
                        eAd[par] = eA
                        wt("scalar", ST, tscB)
                        wt("scalar", expB_free[par][0], expB_free[par][1])
                        eB = op("scalar", lambda e, pa=par, pp=p:
                                e.activation(out=expB[:, pa, :], in_=PSB[:],
                                             func=AF.Exp,
                                             bias=negm_sb[:, 2 * pp + 1:
                                                          2 * pp + 2],
                                             scale=0.125), SS)
                        setfree("B0", SS, eB)
                        setfree("B1", SS, eB)
                        eBd[par] = eB
                        pend = (tb, jp)
                    al = emit_attnv(pend)
                    emit_norm(pend[0], al)
                yh_all = SV.n
                if DEBUG and b == 0 and half == 0:
                    wt("sync", SV, yh_all)
                    op("sync", lambda e: e.dma_start(out=taps["tapw"], in_=wvs_sb[:]), SO, 16)
                    op("sync", lambda e: e.dma_start(out=taps["tapyh0"], in_=yhat), SO, 16)
                first = True
                for tb in range(2):
                    for m in range(8):
                        s = (tb * 8 + m) % 4
                        waitfree("tensor", SLOT_RG[s])
                        if first:
                            wt("tensor", SV, yh_all)
                            if beff_done:
                                wt("tensor", SV, beff_done)
                            first = False
                        t = None
                        for k in range(8):
                            t = op("tensor", lambda e, kk=k, mm=m, t2=tb,
                                   ss=s: e.matmul(
                                       SLOT[ss],
                                       wvsv[:, kk, mm * 128:(mm + 1) * 128],
                                       yhat[:, kk, t2 * 512:(t2 + 1) * 512],
                                       start=(kk == 0), stop=(kk == 7)),
                                   ST if k == 7 else None)
                        wt("vector", ST, t)
                        cv = op("vector", lambda e, mm=m, t2=tb, ss=s, h2=hoff:
                                e.scalar_tensor_tensor(
                                    out=x3[:, mm, h2 + t2 * 512:
                                           h2 + (t2 + 1) * 512],
                                    in0=SLOT[ss],
                                    scalar=beff_sb[:, mm:mm + 1],
                                    in1=x3[:, mm, h2 + t2 * 512:
                                           h2 + (t2 + 1) * 512],
                                    op0=ALU.add, op1=ALU.add), SV)
                        setfree(SLOT_RG[s], SV, cv)
                        x_upd[(b, "D", 2 * half + tb)] = (SV, cv)
            wt("gpsimd", ST, ST.n)
            wv_reload[0] = op("gpsimd", lambda e: e.dma_start(
                out=wvsv, in_=d_wv.rearrange("(k p) d -> p k d", p=128)),
                SDW, 16)

        def tap_dump(nm, src_ap, gates):
            if not DEBUG:
                return
            for g in gates:
                if g:
                    wt("sync", g[0], g[1])
            dst = taps[nm]
            if nm in ("tapA", "tapB", "tapCD"):
                dst = dst.rearrange("(ct p) t -> p ct t", p=128)
            op("sync", lambda e, d=dst, s2=src_ap: e.dma_start(out=d, in_=s2),
               SO, 16)

        for b in range(NB):
            ffn_phase(b, is_E=False)
            if b == 0:
                tap_dump("tapA", x3, [x_upd[(0, "A", 3)]])
                tap_dump("tapxr", xr_bf.rearrange("p a b -> p (a b)") if False else xr_bf,
                         [(SV, SV.n)])
                tap_dump("tapxk", xk_bf, [(SV, SV.n)])
            if not SKIP_ATTN:
                beff_done = phase_b(b)
            if DEBUG and b == 0 and not SKIP_ATTN:
                tap_dump("tapB", x3, [(SS, SS.n), (SV, SV.n)])
                tap_dump("tapckv", ckv, [(SS, SS.n)])
                tap_dump("tapcq", cq_bf, [(SS, SS.n)])
            if not SKIP_ATTN:
                phase_cd(b, beff_done)
            if b == 0:
                tap_dump("tapCD", x3, [(SV, SV.n)])
                tap_dump("tapnegm", negm_sb[:], [(SV, SV.n)])
                tap_dump("tapyh", yhat, [(SV, SV.n)])
                tap_dump("tapq", qhalf, [(SS, SS.n)])
            ffn_phase(b, is_E=True)

        wt("sync", SO, SO.n)

        with nc.Block() as block:
            @block.sync
            def _(e):
                for fn in streams["sync"]:
                    fn(e)

            @block.tensor
            def _(e):
                for fn in streams["tensor"]:
                    fn(e)

            @block.vector
            def _(e):
                for fn in streams["vector"]:
                    fn(e)

            @block.scalar
            def _(e):
                for fn in streams["scalar"]:
                    fn(e)

            @block.gpsimd
            def _(e):
                for fn in streams["gpsimd"]:
                    fn(e)

    return nc


def kernel(**inputs):
    g = {k: np.asarray(v) for k, v in inputs.items()}
    x = g["L0_attn_x"]
    xT = np.ascontiguousarray(x.transpose(0, 2, 1)).astype(np.float32, copy=False)
    wkv = g["L0_attn_wkv_d"]
    wkvrT = np.ascontiguousarray(np.concatenate([wkv.T, wkv.T], axis=1))
    bkvr = np.ascontiguousarray(
        np.concatenate([g["L0_attn_bkv_d"], g["L0_attn_bkv_d"]]))
    common = dict(
        wrT=np.ascontiguousarray(g["L0_ffn_W_r"].T),
        wkT=np.ascontiguousarray(g["L0_ffn_W_k"].T),
        wvT=np.ascontiguousarray(g["L0_ffn_W_v"].T),
        wqT=np.ascontiguousarray(g["L0_attn_wq_d"].T),
        wqkT=np.ascontiguousarray(g["L0_attn_w_qk"].T),
        wkvrT=wkvrT,
        wvuT=np.ascontiguousarray(g["L0_attn_wv_u"].T),
        wpjT=np.ascontiguousarray(g["L0_attn_w_proj"].T),
        mur=np.ascontiguousarray(g["L0_ffn_mu_r"].reshape(8, 128).T),
        muk=np.ascontiguousarray(g["L0_ffn_mu_k"].reshape(8, 128).T),
        bq=np.ascontiguousarray(g["L0_attn_bq_d"].reshape(4, 128).T),
        bqk=np.ascontiguousarray(g["L0_attn_b_qk"].reshape(8, 128).T),
        bkvr=np.ascontiguousarray(bkvr.reshape(1, 128).T),
        bvu=np.ascontiguousarray(g["L0_attn_bv_u"].reshape(8, 128).T),
        bpj=np.ascontiguousarray(g["L0_attn_b_proj"].reshape(8, 128).T),
        idn=np.eye(128).astype(ml_dtypes.bfloat16),
    )
    in_maps = [dict(common, xT=np.ascontiguousarray(xT[2 * i:2 * i + 2]))
               for i in range(8)]
    nc = build_program()
    res = run_bass_kernel_spmd(nc, in_maps, core_ids=list(range(8)))
    if DEBUG:
        global TAPS
        TAPS = {k: np.asarray(v) for k, v in res.results[0].items()}
    out = np.concatenate([r["outT"] for r in res.results], axis=0)
    return np.ascontiguousarray(out.transpose(0, 2, 1))


if __name__ == "__main__":
    import reference as R
    ins = R.setup_inputs()
    out = kernel(**{k: np.asarray(v) for k, v in ins.items()})
    ref = np.asarray(R.reference(**{k: np.asarray(v) for k, v in ins.items()}))
    rv = float(np.mean((out - ref) ** 2) / np.mean(ref ** 2))
    print("maxabs", np.abs(out - ref).max(), "resvar", rv)


# version 10
# speedup vs baseline: 1.3874x; 1.0046x over previous
import numpy as np
import ml_dtypes
from contextlib import ExitStack

import concourse.bass as bass
import concourse.mybir as mybir
from concourse.bass_utils import run_bass_kernel_spmd

F32 = mybir.dt.float32
BF16 = mybir.dt.bfloat16
AF = mybir.ActivationFunctionType
ALU = mybir.AluOpType
AX = mybir.AxisListType

T = 2048
C = 1024
NB = 2
NCH = 4
CHK = 512
NP = 8
EPS = 1e-5
ENGS = ("tensor", "vector", "scalar", "gpsimd", "sync")
DEBUG = False
SKIP_ATTN = False
TAPS = {}


def build_program():
    nc = bass.Bass("TRN2", target_bir_lowering=False)

    d_x = nc.dram_tensor("xT", [NB, C, T], F32, kind="ExternalInput").ap()
    d_out = nc.dram_tensor("outT", [NB, C, T], F32, kind="ExternalOutput").ap()
    d_wr = nc.dram_tensor("wrT", [C, C], F32, kind="ExternalInput").ap()
    d_wk = nc.dram_tensor("wkT", [C, C], F32, kind="ExternalInput").ap()
    d_wv = nc.dram_tensor("wvT", [C, C], F32, kind="ExternalInput").ap()
    d_wq = nc.dram_tensor("wqT", [C, 512], F32, kind="ExternalInput").ap()
    d_wqk = nc.dram_tensor("wqkT", [512, C], F32, kind="ExternalInput").ap()
    d_wkvr = nc.dram_tensor("wkvrT", [C, 128], F32, kind="ExternalInput").ap()
    d_wvu = nc.dram_tensor("wvuT", [64, C], F32, kind="ExternalInput").ap()
    d_wpj = nc.dram_tensor("wpjT", [C, C], F32, kind="ExternalInput").ap()
    d_mur = nc.dram_tensor("mur", [128, 8], F32, kind="ExternalInput").ap()
    d_muk = nc.dram_tensor("muk", [128, 8], F32, kind="ExternalInput").ap()
    d_bq = nc.dram_tensor("bq", [128, 4], F32, kind="ExternalInput").ap()
    d_bqk = nc.dram_tensor("bqk", [128, 8], F32, kind="ExternalInput").ap()
    d_bkvr = nc.dram_tensor("bkvr", [128, 1], F32, kind="ExternalInput").ap()
    d_bvu = nc.dram_tensor("bvu", [128, 8], F32, kind="ExternalInput").ap()
    d_bpj = nc.dram_tensor("bpj", [128, 8], F32, kind="ExternalInput").ap()
    d_idn = nc.dram_tensor("idn", [128, 128], BF16, kind="ExternalInput").ap()
    taps = {}
    if DEBUG:
        for nm, shp, dt in (("tapA", [C, T], F32), ("tapB", [C, T], F32),
                            ("tapCD", [C, T], F32), ("tapckv", [128, T], BF16),
                            ("tapq", [128, 8 * 1024], BF16),
                            ("tapyh", [128, 8 * 1024], BF16),
                            ("tapnegm", [128, 16], F32),
                            ("tapcq", [128, 4 * T], BF16),
                            ("tapxr", [128, 8 * 512], BF16),
                            ("tapxk", [128, 8 * 512], BF16),
                            ("tapr0", [128, 8 * 512], BF16),
                            ("tapka0", [128, 8 * 512], BF16),
                            ("tapxh0", [128, 8 * 513], F32),
                            ("tapx0", [128, 8 * 512], F32),
                            ("tapx1", [128, 8 * 512], F32),
                            ("tapw", [128, 8 * 1024], BF16),
                            ("tapyh0", [128, 8 * 1024], BF16)):
            taps[nm] = nc.dram_tensor(nm, shp, dt, kind="ExternalOutput").ap()

    streams = {k: [] for k in ENGS}

    with ExitStack() as ctx:
        sb = lambda n, s, d: ctx.enter_context(nc.sbuf_tensor(n, s, d))

        x_sb = sb("x_sb", [128, 8 * T], F32)
        wr_sb = sb("wr_sb", [128, 8 * 1024], BF16)
        wk_sb = sb("wk_sb", [128, 8 * 1024], BF16)
        wvs_sb = sb("wvs_sb", [128, 8 * 1024], BF16)
        wq_sb = sb("wq_sb", [128, 8 * 512], BF16)
        wqk_sb = sb("wqk_sb", [128, 4 * 1024], BF16)
        wkvr_sb = sb("wkvr_sb", [128, 8 * 128], BF16)
        wvu_sb = sb("wvu_sb", [64, 1024], BF16)
        arena = sb("arena", [128, 37392], BF16)

        mur_sb = sb("mur_sb", [128, 8], F32)
        muk_sb = sb("muk_sb", [128, 8], F32)
        bq_sb = sb("bq_sb", [128, 4], F32)
        bqk_sb = sb("bqk_sb", [128, 8], F32)
        bkvr_sb = sb("bkvr_sb", [128, 1], F32)
        bvu_bf = sb("bvu_bf", [128, 8], BF16)
        bpj_sb = sb("bpj_sb", [128, 8], F32)
        beff_sb = sb("beff_sb", [128, 8], F32)
        eps_sb = sb("eps_sb", [128, 1], F32)
        idn_sb = sb("idn_sb", [128, 128], BF16)
        ones1 = sb("ones1", [128, 128], BF16)
        onesm = sb("onesm", [128, 128], BF16)
        negm_sb = sb("negm_sb", [128, 16], F32)
        pmax2 = sb("pmax2", [128, 2], BF16)
        m2b = sb("m2b", [128, 1], BF16)
        rowb = sb("rowb", [128, 2], BF16)

        PSA = ctx.enter_context(nc.psum_tensor("PSA", [128, 1024], F32))
        PSB = ctx.enter_context(nc.psum_tensor("PSB", [128, 1024], F32))
        PSY = ctx.enter_context(nc.psum_tensor("PSY", [128, 1024], F32))
        PSM = ctx.enter_context(nc.psum_tensor("PSM", [128, 512], F32))
        PSBF = ctx.enter_context(nc.psum_tensor("PSBF", [128, 256], BF16))

        SLOT = [PSA[:, 0:512], PSA[:, 512:1024], PSB[:, 0:512],
                PSB[:, 512:1024], PSY[:, 0:512]]
        SLOT_RG = ["A0", "A1", "B0", "B1", "Y0"]
        MSQ = PSY[:, 512:1024]

        arb = arena[:]
        arf = arena[:].bitcast(F32)

        def vb(off, shape):
            n = int(np.prod(shape[1:]))
            a = arb[:, off // 2: off // 2 + n]
            if len(shape) == 3:
                a = a.rearrange("p (a b) -> p a b", a=shape[1])
            elif len(shape) == 4:
                a = a.rearrange("p (a b c) -> p a b c", a=shape[1], b=shape[2])
            return a

        def vf(off, shape):
            n = int(np.prod(shape[1:]))
            a = arf[:, off // 4: off // 4 + n]
            if len(shape) == 3:
                a = a.rearrange("p (a b) -> p a b", a=shape[1])
            return a

        xr_bf = vb(0, [128, 8, 512])
        xbf2 = vb(8192, [128, 8, 512])
        xsq = vb(16384, [128, 8, 512])
        gpd = vf(16384, [128, 512])
        mean_sb = vf(24576, [128, 512])
        var_sb = vf(26624, [128, 512])
        tmpv = vf(28672, [128, 512])
        qhalf = vb(0, [128, 8, 1024])
        vaug = vb(16384, [128, 16, 256])
        vaug4 = vb(16384, [128, 16, 4, 64])
        expA = vb(24576, [128, 2, 1024])
        xhat = vf(30720, [128, 8, 513])
        xk_bf = vb(47136, [128, 8, 512])
        kact = vb(55328, [128, 8, 512])
        r_bf = vb(63520, [128, 8, 512])
        dbuf = vf(71712, [128, 512])
        relu2 = vb(73760, [128, 512])
        cq_bf = vb(30720, [128, 4, 2048])
        ckv = vb(47104, [128, 2048])
        yhat = vb(51200, [128, 8, 1024])
        expB = vb(67584, [128, 2, 1024])
        recmv = vb(71680, [128, 512])

        x3 = x_sb[:].rearrange("p (ct t) -> p ct t", ct=8)
        wrv = wr_sb[:].rearrange("p (k d) -> p k d", k=8)
        wkv_ = wk_sb[:].rearrange("p (k d) -> p k d", k=8)
        wvsv = wvs_sb[:].rearrange("p (k d) -> p k d", k=8)
        wqv = wq_sb[:].rearrange("p (k d) -> p k d", k=8)
        wqkv = wqk_sb[:].rearrange("p (k d) -> p k d", k=4)
        wkvrv = wkvr_sb[:].rearrange("p (k d) -> p k d", k=8)

        class Sem:
            def __init__(self, name):
                self.h = ctx.enter_context(nc.semaphore(name))
                self.n = 0

        ST, SS, SV, SG = Sem("ST"), Sem("SS"), Sem("SV"), Sem("SG")
        SDW, SDX, SO = Sem("SDW"), Sem("SDX"), Sem("SO")

        def op(eng, build, sem=None, k=1):
            if sem is None:
                streams[eng].append(build)
                return None
            sem.n += k
            streams[eng].append(
                lambda e, b=build, hh=sem.h, kv=k: b(e).then_inc(hh, kv))
            return sem.n

        def wt(eng, sem, val):
            if sem is None or val is None or val <= 0:
                return
            streams[eng].append(lambda e, hh=sem.h, v=val: e.wait_ge(hh, v))

        FREE = {r: (SV, 0) for r in
                ("A0", "A1", "B0", "B1", "Y0", "Y1", "M", "BF")}

        def waitfree(eng, *regions):
            for r in regions:
                s, v = FREE[r]
                wt(eng, s, v)

        def setfree(region, sem, val):
            FREE[region] = (sem, val)

        xin_ch = [[0] * NCH for _ in range(NB)]

        def emit_xload(b, ch):
            src = d_x[b].rearrange("(ct p) t -> p ct t", p=128)[
                :, :, ch * CHK:(ch + 1) * CHK]
            dst = x3[:, :, ch * CHK:(ch + 1) * CHK]
            xin_ch[b][ch] = op("sync",
                               lambda e, s=src, d=dst: e.dma_start(out=d, in_=s),
                               SDX, 16)

        for ch in range(NCH):
            emit_xload(0, ch)
        for dst, src in ((mur_sb, d_mur), (muk_sb, d_muk), (bq_sb, d_bq),
                         (bqk_sb, d_bqk), (bkvr_sb, d_bkvr), (bpj_sb, d_bpj),
                         (idn_sb, d_idn)):
            op("sync", lambda e, d=dst, s=src: e.dma_start(out=d[:], in_=s),
               SDX, 16)
        sdx_smalls = SDX.n

        wdma = {}

        def emit_wload(name, dst, src):
            wdma[name] = op("gpsimd",
                            lambda e, d=dst, s=src: e.dma_start(out=d, in_=s),
                            SDW, 16)

        emit_wload("wr", wrv, d_wr.rearrange("(k p) d -> p k d", p=128))
        emit_wload("wk", wkv_, d_wk.rearrange("(k p) d -> p k d", p=128))
        emit_wload("wv0", wvsv, d_wv.rearrange("(k p) d -> p k d", p=128))
        emit_wload("wkvr", wkvrv, d_wkvr.rearrange("(k p) d -> p k d", p=128))
        emit_wload("wq", wqv, d_wq.rearrange("(k p) d -> p k d", p=128))
        emit_wload("wqk", wqkv, d_wqk.rearrange("(k p) d -> p k d", p=128))
        emit_wload("wvu", wvu_sb[:], d_wvu)
        emit_wload("bvu", bvu_bf[:], d_bvu)

        op("vector", lambda e: e.memset(ones1[:], 1.0))
        op("vector", lambda e: e.memset(onesm[:], 1.0 / 1024.0))
        op("vector", lambda e: e.memset(eps_sb[:], EPS))
        op("vector", lambda e: e.memset(m2b[:], 0.0))
        op("vector", lambda e: e.memset(pmax2[:], 0.0))
        v_ones = op("vector", lambda e: e.memset(m2b[:], 0.0), SV)
        for _eng in ("tensor", "scalar", "gpsimd"):
            wt(_eng, SV, v_ones)
            wt(_eng, SDX, sdx_smalls)
        wt("vector", SDX, sdx_smalls)

        stats_gate = [0, 0]
        x_upd = {}
        wv_reload = [0]
        si = [0]
        slot_free = [(SV, 0)] * 5
        probe_prev = {"t1": 0, "t2": 0, "t3": 0}
        negm_gate = [0]
        xsq_gate = [0]
        psm_recmv_cons = [0]
        last_tmv = [0]
        kvev = [0] * 8

        def stats_stage(b, ch, xgate):
            cc = slice(ch * CHK, (ch + 1) * CHK)
            wt("gpsimd", SDX, xin_ch[b][ch])
            if xgate:
                wt("gpsimd", xgate[0], xgate[1])
            wt("gpsimd", ST, stats_gate[0])
            g_cast = op("gpsimd", lambda e, c=cc: e.tensor_copy(
                out=xbf2[:, :, :], in_=x3[:, :, c]), SG)
            wt("scalar", SDX, xin_ch[b][ch])
            if xgate:
                wt("scalar", xgate[0], xgate[1])
            wt("scalar", ST, stats_gate[1])
            wt("scalar", SV, xsq_gate[0])
            s_xsq = op("scalar", lambda e, c=cc: e.activation(
                out=xsq[:, :, :], in_=x3[:, :, c], func=AF.Square), SS)

            wt("tensor", SG, g_cast)
            wt("tensor", SS, s_xsq)
            waitfree("tensor", "M", "Y1")
            t_mean = None
            for k in range(8):
                t_mean = op("tensor", lambda e, kk=k: e.matmul(
                    PSM[:], onesm[:], xbf2[:, kk, :],
                    start=(kk == 0), stop=(kk == 7)), ST if k == 7 else None)
            t_msq = None
            for k in range(8):
                t_msq = op("tensor", lambda e, kk=k: e.matmul(
                    MSQ, onesm[:], xsq[:, kk, :],
                    start=(kk == 0), stop=(kk == 7)), ST if k == 7 else None)
            stats_gate[0] = t_mean
            stats_gate[1] = t_msq

            wt("scalar", ST, t_mean)
            s_mean = op("scalar", lambda e: e.activation(
                out=mean_sb[:], in_=PSM[:], func=AF.Identity), SS)
            setfree("M", SS, s_mean)
            s_m2 = op("scalar", lambda e: e.activation(
                out=var_sb[:], in_=mean_sb[:], func=AF.Square), SS)
            wt("vector", ST, t_msq)
            wt("vector", SS, s_m2)
            v_var = op("vector", lambda e: e.tensor_tensor(
                out=var_sb[:], in0=MSQ, in1=var_sb[:], op=ALU.subtract), SV)
            setfree("Y1", SV, v_var)
            wt("scalar", SV, v_var)
            op("scalar", lambda e: e.activation(
                out=var_sb[:], in_=var_sb[:], func=AF.Ln,
                bias=eps_sb[:, 0:1], scale=1.0))
            s_rstd = op("scalar", lambda e: e.activation(
                out=var_sb[:], in_=var_sb[:], func=AF.Exp,
                bias=0.0, scale=-0.5), SS)
            return dict(rstd=s_rstd, cc=cc)

        VCT, GCT = (0, 1, 2, 3), (4, 5, 6, 7)
        gpd_ct = {c: vf(16384 + 2048 * i, [128, 512]) for i, c in enumerate(GCT)}

        def mix_stage(b, ch, st, r_gate, k_gate):
            cc = st["cc"]
            wt("gpsimd", SS, st["rstd"])
            wt("gpsimd", ST, max(r_gate or 0, k_gate or 0))
            wt("gpsimd", SV, xsq_gate[0])
            g_d = {}
            for ct in GCT:
                if ch == 0:
                    op("gpsimd", lambda e, c=ct: e.memset(xhat[:, c, 0:1], 0.0))
                else:
                    op("gpsimd", lambda e, c=ct: e.tensor_copy(
                        out=xhat[:, c, 0:1], in_=xhat[:, c, 512:513]))
                op("gpsimd", lambda e, c=ct, s=cc: e.tensor_tensor(
                    out=xhat[:, c, 1:513], in0=x3[:, c, s], in1=mean_sb[:],
                    op=ALU.subtract))
                op("gpsimd", lambda e, c=ct: e.tensor_tensor(
                    out=xhat[:, c, 1:513], in0=xhat[:, c, 1:513],
                    in1=var_sb[:], op=ALU.mult))
                g_d[ct] = op("gpsimd", lambda e, c=ct: e.tensor_tensor(
                    out=gpd_ct[c][:], in0=xhat[:, c, 1:513],
                    in1=xhat[:, c, 0:512], op=ALU.subtract), SG)
            wt("vector", SS, st["rstd"])
            wt("vector", ST, r_gate)
            wt("vector", ST, k_gate)
            v_last = None
            for ct in VCT:
                if ch == 0:
                    op("vector", lambda e, c=ct: e.memset(xhat[:, c, 0:1], 0.0))
                else:
                    op("vector", lambda e, c=ct: e.tensor_copy(
                        out=xhat[:, c, 0:1], in_=xhat[:, c, 512:513]))
                op("vector", lambda e, c=ct, s=cc: e.tensor_tensor(
                    out=xhat[:, c, 1:513], in0=x3[:, c, s], in1=mean_sb[:],
                    op=ALU.subtract))
                op("vector", lambda e, c=ct: e.tensor_tensor(
                    out=xhat[:, c, 1:513], in0=xhat[:, c, 1:513],
                    in1=var_sb[:], op=ALU.mult))
                op("vector", lambda e, c=ct: e.tensor_tensor(
                    out=dbuf[:], in0=xhat[:, c, 1:513], in1=xhat[:, c, 0:512],
                    op=ALU.subtract))
                op("vector", lambda e, c=ct: e.scalar_tensor_tensor(
                    out=xr_bf[:, c, :], in0=dbuf[:], scalar=mur_sb[:, c:c + 1],
                    in1=xhat[:, c, 0:512], op0=ALU.mult, op1=ALU.add))
                v_last = op("vector", lambda e, c=ct: e.scalar_tensor_tensor(
                    out=xk_bf[:, c, :], in0=dbuf[:], scalar=muk_sb[:, c:c + 1],
                    in1=xhat[:, c, 0:512], op0=ALU.mult, op1=ALU.add), SV)
            for ct in GCT:
                wt("vector", SG, g_d[ct])
                op("vector", lambda e, c=ct: e.scalar_tensor_tensor(
                    out=xr_bf[:, c, :], in0=gpd_ct[c][:],
                    scalar=mur_sb[:, c:c + 1], in1=xhat[:, c, 0:512],
                    op0=ALU.mult, op1=ALU.add))
                v_last = op("vector", lambda e, c=ct: e.scalar_tensor_tensor(
                    out=xk_bf[:, c, :], in0=gpd_ct[c][:],
                    scalar=muk_sb[:, c:c + 1], in1=xhat[:, c, 0:512],
                    op0=ALU.mult, op1=ALU.add), SV)
            xsq_gate[0] = v_last
            return {"vector": v_last, "gpsimd": g_d[GCT[-1]]}

        def mm_group(w_view, rhs_view, m, wgate=None):
            s = si[0] % 5
            si[0] += 1
            sl = SLOT[s]
            waitfree("tensor", SLOT_RG[s])
            if wgate:
                wt("tensor", SDW, wgate)
            t = None
            for k in range(8):
                t = op("tensor", lambda e, kk=k, sl_=sl, w=w_view, r=rhs_view,
                       mm=m: e.matmul(sl_, w[:, kk, mm * 128:(mm + 1) * 128],
                                      r[:, kk, :], start=(kk == 0),
                                      stop=(kk == 7)), ST if k == 7 else None)
            return s, sl, t

        def ffn_phase(b, is_E):
            ph = "E" if is_E else "A"
            gate_all = ST.n
            stats_gate[0] = max(stats_gate[0], gate_all)
            stats_gate[1] = max(stats_gate[1], gate_all)

            def xgate(ch):
                if is_E:
                    if SKIP_ATTN:
                        return x_upd.get((b, "A", ch))
                    return x_upd.get((b, "D", ch))
                return None

            st = stats_stage(b, 0, xgate(0))
            mix = mix_stage(b, 0, st, gate_all, gate_all)
            r_last = [gate_all]
            k_last = [gate_all]
            for ch in range(NCH):
                cc = slice(ch * CHK, (ch + 1) * CHK)
                wt("tensor", SV, mix["vector"])
                wt("tensor", SG, mix["gpsimd"])
                t_r = None
                for m in range(8):
                    s, sl, t = mm_group(wrv, xr_bf, m,
                                        wdma["wr"] if (b == 0 and not is_E and
                                                       ch == 0) else None)
                    t_r = t
                    wt("scalar", ST, t)
                    wt("scalar", SV, kvev[m])
                    c = op("scalar", lambda e, sl_=sl, mm=m: e.activation(
                        out=r_bf[:, mm, :], in_=sl_, func=AF.Sigmoid), SS)
                    setfree(SLOT_RG[s], SS, c)
                r_last[0] = t_r
                if ch + 1 < NCH:
                    st2 = stats_stage(b, ch + 1, xgate(ch + 1))
                t_k = None
                for m in range(8):
                    s, sl, t = mm_group(wkv_, xk_bf, m,
                                        wdma["wk"] if (b == 0 and not is_E and
                                                       ch == 0) else None)
                    t_k = t
                    wt("scalar", ST, t)
                    c = op("scalar", lambda e, sl_=sl, mm=m: e.activation(
                        out=kact[:, mm, :], in_=sl_, func=AF.Relu), SS)
                    wt("vector", SS, c)
                    cv = op("vector", lambda e, sl_=sl, mm=m:
                            e.tensor_tensor(out=kact[:, mm, :],
                                            in0=kact[:, mm, :],
                                            in1=sl_, op=ALU.mult), SV)
                    setfree(SLOT_RG[s], SV, cv)
                k_last[0] = t_k
                kvgroups = []
                for m in range(8):
                    wg = None
                    if is_E and ch == 0 and m == 0:
                        wg = wv_reload[0]
                    elif b == 0 and not is_E and ch == 0:
                        wg = wdma["wv0"]
                    s, sl, t = mm_group(wvsv, kact, m, wg)
                    kvgroups.append((m, s, sl, t))

                def kv_evac(m, s, sl, t):
                    wt("vector", ST, t)
                    op("vector", lambda e, sl_=sl, mm=m: e.scalar_tensor_tensor(
                        out=tmpv[:], in0=sl_, scalar=0.5, in1=r_bf[:, mm, :],
                        op0=ALU.mult, op1=ALU.mult))
                    cv = op("vector", lambda e, mm=m, c2=cc: e.tensor_tensor(
                        out=x3[:, mm, c2], in0=x3[:, mm, c2], in1=tmpv[:],
                        op=ALU.add), SV)
                    setfree(SLOT_RG[s], SV, cv)
                    kvev[m] = cv
                    return cv

                xdone = None
                for g in kvgroups[:5]:
                    xdone = kv_evac(*g)
                if ch + 1 < NCH:
                    mix = mix_stage(b, ch + 1, st2, r_last[0], k_last[0])
                    st = st2
                for g in kvgroups[5:]:
                    xdone = kv_evac(*g)
                x_upd[(b, ph, ch)] = (SV, xdone)
                if DEBUG and b == 0 and not is_E and ch == 0:
                    wt("sync", SV, xdone)
                    op("sync", lambda e: e.dma_start(out=taps["tapr0"], in_=r_bf), SO, 16)
                    op("sync", lambda e: e.dma_start(out=taps["tapka0"], in_=kact), SO, 16)
                    op("sync", lambda e: e.dma_start(out=taps["tapxh0"], in_=xhat), SO, 16)
                    op("sync", lambda e: e.dma_start(out=taps["tapx0"], in_=x3[:, :, 0:512]), SO, 16)
                if DEBUG and b == 0 and not is_E and ch == 1:
                    wt("sync", SV, xdone)
                    op("sync", lambda e: e.dma_start(out=taps["tapx1"], in_=x3[:, :, 512:1024]), SO, 16)
                if is_E:
                    wt("sync", SV, xdone)
                    so = op("sync", lambda e, bb=b, c2=cc, c=ch: e.dma_start(
                        out=d_out[bb].rearrange("(ct p) t -> p ct t", p=128)[
                            :, :, c * CHK:(c + 1) * CHK],
                        in_=x3[:, :, c2]), SO, 16)
                    if b == 0:
                        wt("sync", SO, so)
                        emit_xload(1, ch)

        def phase_b(b):
            wt("gpsimd", ST, ST.n)
            wdma["wpj"] = op("gpsimd", lambda e: e.dma_start(
                out=wvsv, in_=d_wpj.rearrange("(k p) d -> p k d", p=128)),
                SDW, 16)
            prev_ck = 0
            for ch in range(NCH):
                cc = slice(ch * CHK, (ch + 1) * CHK)
                st = stats_stage(b, ch, x_upd[(b, "A", ch)])
                wt("vector", SS, st["rstd"])
                wt("vector", ST, prev_ck)
                xhb = None
                for ct in range(8):
                    op("vector", lambda e, c=ct, s=cc: e.tensor_tensor(
                        out=tmpv[:], in0=x3[:, c, s], in1=mean_sb[:],
                        op=ALU.subtract))
                    xhb = op("vector", lambda e, c=ct: e.tensor_tensor(
                        out=xr_bf[:, c, :], in0=tmpv[:], in1=var_sb[:],
                        op=ALU.mult), SV)
                wt("tensor", SV, xhb)
                for mq in range(4):
                    waitfree("tensor", SLOT_RG[mq])
                    if b == 0 and ch == 0 and mq == 0:
                        wt("tensor", SDW, wdma["wq"])
                    t = None
                    for k in range(8):
                        t = op("tensor", lambda e, kk=k, m=mq: e.matmul(
                            SLOT[m], wqv[:, kk, m * 128:(m + 1) * 128],
                            xr_bf[:, kk, :], start=(kk == 0), stop=(kk == 7)),
                            ST if k == 7 else None)
                    wt("scalar", ST, t)
                    c = op("scalar", lambda e, m=mq, s=cc: e.activation(
                        out=cq_bf[:, m, s], in_=SLOT[m], func=AF.Identity,
                        bias=bq_sb[:, m:m + 1], scale=1.0), SS)
                    setfree(SLOT_RG[mq], SS, c)
                waitfree("tensor", "Y0")
                if b == 0 and ch == 0:
                    wt("tensor", SDW, wdma["wkvr"])
                t = None
                for k in range(8):
                    t = op("tensor", lambda e, kk=k: e.matmul(
                        SLOT[4], wkvrv[:, kk, :], xr_bf[:, kk, :],
                        start=(kk == 0), stop=(kk == 7)), ST if k == 7 else None)
                prev_ck = t
                wt("scalar", ST, t)
                c = op("scalar", lambda e, s=cc: e.activation(
                    out=ckv[:, s], in_=SLOT[4], func=AF.Identity,
                    bias=bkvr_sb[:, 0:1], scale=1.0), SS)
                setfree("Y0", SS, c)
            if b == 0:
                wt("tensor", SDW, wdma["wpj"])
                wt("tensor", SDW, wdma["bvu"])
                waitfree("tensor", "M")
                t = None
                for m in range(8):
                    for k in range(8):
                        t = op("tensor", lambda e, kk=k, mm=m: e.matmul(
                            PSM[:, mm:mm + 1],
                            wvsv[:, kk, mm * 128:(mm + 1) * 128],
                            bvu_bf[:, kk:kk + 1], start=(kk == 0),
                            stop=(kk == 7)),
                            ST if (k == 7 and m == 7) else None)
                wt("vector", ST, t)
                v = op("vector", lambda e: e.tensor_tensor(
                    out=beff_sb[:], in0=PSM[:, 0:8], in1=bpj_sb[:],
                    op=ALU.add), SV)
                setfree("M", SV, v)
                return v
            return None

        def probe_chain(p, half):
            waitfree("tensor", "M")
            wt("tensor", ST, probe_prev["t1"])
            tA = op("tensor", lambda e, pp=p, h=half: e.matmul(
                PSM[:], ckv[0:64, 0:128],
                qhalf[0:64, pp, 0:512],
                start=True, stop=True, tile_position=(0, 0)), ST)
            wt("vector", ST, tA)
            vA = op("vector", lambda e: e.reduce_max(
                out=pmax2[:, 0:1], in_=PSM[:], axis=AX.X), SV)
            wt("tensor", SV, vA)
            tB = op("tensor", lambda e, pp=p: e.matmul(
                PSM[:], ckv[64:128, 0:128], qhalf[64:128, pp, 0:512],
                start=True, stop=True, tile_position=(64, 0)), ST)
            wt("vector", ST, tB)
            vB = op("vector", lambda e: e.reduce_max(
                out=pmax2[:, 1:2], in_=PSM[:], axis=AX.X), SV)
            waitfree("tensor", "BF")
            wt("tensor", SV, vB)
            t1 = op("tensor", lambda e: e.transpose(
                out=PSBF[0:2, 0:128], in_=pmax2[:], identity=idn_sb[:]), ST)
            probe_prev["t1"] = t1
            wt("vector", ST, t1)
            wt("vector", ST, probe_prev["t2"])
            v2 = op("vector", lambda e: e.reduce_max(
                out=m2b[0:2, 0:1], in_=PSBF[0:2, 0:128], axis=AX.X), SV)
            wt("tensor", SV, v2)
            t2 = op("tensor", lambda e: e.transpose(
                out=PSBF[0:1, 128:256], in_=m2b[:], identity=idn_sb[:]), ST)
            probe_prev["t2"] = t2
            wt("scalar", ST, t2)
            wt("scalar", ST, probe_prev["t3"])
            s1 = op("scalar", lambda e: e.activation(
                out=rowb[0:1, 0:2], in_=PSBF[0:1, 128:130],
                func=AF.Identity), SS)
            setfree("BF", SS, s1)
            wt("tensor", SS, s1)
            t3 = op("tensor", lambda e: e.matmul(
                PSM[:, 0:2], ones1[0:1, :], rowb[0:1, 0:2],
                start=True, stop=True), ST)
            probe_prev["t3"] = t3
            wt("vector", ST, t3)
            wt("vector", SS, negm_gate[0])
            v3 = op("vector", lambda e, pp=p: e.tensor_scalar(
                out=negm_sb[:, 2 * pp:2 * pp + 2], in0=PSM[:, 0:2],
                scalar1=-0.125, scalar2=None, op0=ALU.mult), SV)
            setfree("M", SV, v3)
            return v3

        def phase_cd(b, beff_done):
            negm_ready = [0] * NP
            negm_gate[0] = SS.n
            cq_ready = SS.n
            for half in range(2):
                hoff = half * 1024
                if b == 0 and half == 0:
                    wt("tensor", SDW, wdma["wqk"])
                wt("tensor", SS, cq_ready)
                qdone = None
                for p in range(NP):
                    sidx = p % 4
                    for tb2 in range(2):
                        waitfree("tensor", SLOT_RG[sidx])
                        t = None
                        for k in range(4):
                            t = op("tensor", lambda e, kk=k, pp=p, t2=tb2, h2=hoff:
                                   e.matmul(SLOT[pp % 4],
                                            wqkv[:, kk, pp * 128:(pp + 1) * 128],
                                            cq_bf[:, kk, h2 + t2 * 512:
                                                  h2 + (t2 + 1) * 512],
                                            start=(kk == 0), stop=(kk == 3)),
                                   ST if k == 3 else None)
                        wt("scalar", ST, t)
                        qdone = op("scalar", lambda e, pp=p, t2=tb2: e.activation(
                            out=qhalf[:, pp, t2 * 512:(t2 + 1) * 512],
                            in_=SLOT[pp % 4], func=AF.Identity,
                            bias=bqk_sb[:, pp:pp + 1], scale=1.0), SS)
                        setfree(SLOT_RG[sidx], SS, qdone)
                if half == 0:
                    wt("tensor", SS, qdone)
                    negm_ready[0] = probe_chain(0, half)
                expA_free = [(ST, 0), (ST, 0)]
                expB_free = [(ST, 0), (ST, 0)]
                for p in range(NP):
                    vevac = None
                    for rnd in range(2):
                        waitfree("tensor", "Y0", "Y1")
                        if rnd == 0:
                            wt("tensor", SV, v_ones)
                            if b == 0 and half == 0 and p == 0:
                                wt("tensor", SDW, wdma["wvu"])
                        else:
                            wt("tensor", SV, vevac)
                        t = None
                        for s8 in range(8):
                            stile = rnd * 8 + s8
                            t = op("tensor", lambda e, s_=s8, sl_=stile, pp=p:
                                   e.matmul(PSY[:, s_ * 128:(s_ + 1) * 128],
                                            ckv[0:64, sl_ * 128:(sl_ + 1) * 128],
                                            wvu_sb[0:64, pp * 128:(pp + 1) * 128],
                                            start=True, stop=True),
                                   ST if s8 == 7 else None)
                        wt("vector", ST, t)
                        psyv = PSY[:].rearrange("p (s g c) -> p s g c", s=8, g=2)
                        op("vector", lambda e, r=rnd: e.memset(
                            vaug4[:, r * 8:(r + 1) * 8, 1:3, :], 1.0))
                        op("vector", lambda e, r=rnd, pv=psyv: e.tensor_copy(
                            out=vaug4[:, r * 8:(r + 1) * 8, 0, :],
                            in_=pv[:, :, 0, :]))
                        vevac = op("vector", lambda e, r=rnd, pv=psyv:
                                   e.tensor_copy(
                                       out=vaug4[:, r * 8:(r + 1) * 8, 3, :],
                                       in_=pv[:, :, 1, :]), SV)
                    vj_done = vevac
                    setfree("Y0", SV, vj_done)
                    setfree("Y1", SV, vj_done)
                    if half == 0 and p + 1 < NP:
                        negm_ready[p + 1] = probe_chain(p + 1, half)
                    units = [(tb, jp) for tb in range(2) for jp in range(8)]
                    pend = None

                    def emit_attnv(u):
                        tbp, jpp = u
                        parp = (tbp * 8 + jpp) % 2
                        if jpp == 0:
                            wt("tensor", SV, vj_done)
                            waitfree("tensor", "Y0", "Y1")
                        wt("tensor", SS, eAd[parp])
                        aA = None
                        for odd in range(2):
                            stile = 2 * jpp + odd
                            aA = op("tensor", lambda e, s_=stile, o=odd,
                                    pa=parp: e.matmul(
                                        PSY[:, 0:512], vaug[:, s_, 0:128],
                                        expA[:, pa, o * 512:(o + 1) * 512],
                                        start=(s_ == 0), stop=(s_ == 15)),
                                    ST if odd == 1 else None)
                        expA_free[parp] = (ST, aA)
                        wt("tensor", SS, eBd[parp])
                        al = None
                        for odd in range(2):
                            stile = 2 * jpp + odd
                            al = op("tensor", lambda e, s_=stile, o=odd,
                                    pa=parp: e.matmul(
                                        PSY[:, 512:1024], vaug[:, s_, 128:256],
                                        expB[:, pa, o * 512:(o + 1) * 512],
                                        start=(s_ == 0), stop=(s_ == 15)),
                                    ST if odd == 1 else None)
                        expB_free[parp] = (ST, al)
                        return al

                    def emit_norm(tbp, att_last):
                        t0 = tbp * 512
                        wt("scalar", ST, att_last)
                        op("scalar", lambda e: e.activation(
                            out=tmpv[64:128, :], in_=PSY[64:128, 0:512],
                            func=AF.Ln))
                        op("scalar", lambda e: e.activation(
                            out=tmpv[0:64, :], in_=PSY[0:64, 512:1024],
                            func=AF.Ln))
                        wt("scalar", ST, last_tmv[0])
                        wt("scalar", SV, psm_recmv_cons[0])
                        s_rec = op("scalar", lambda e: e.activation(
                            out=recmv[:], in_=tmpv[:], func=AF.Exp,
                            bias=0.0, scale=-1.0), SS)
                        waitfree("tensor", "M")
                        wt("tensor", SS, s_rec)
                        op("tensor", lambda e: e.matmul(
                            PSM[0:64, :], idn_sb[64:128, 64:128],
                            recmv[64:128, :], start=True, stop=True,
                            tile_position=(64, 0)))
                        t_mv = op("tensor", lambda e: e.matmul(
                            PSM[64:128, :], idn_sb[0:64, 0:64], recmv[0:64, :],
                            start=True, stop=True, tile_position=(0, 64)), ST)
                        last_tmv[0] = t_mv
                        wt("vector", ST, t_mv)
                        v_rc = op("vector", lambda e: e.tensor_copy(
                            out=recmv[:], in_=PSM[:]), SV)
                        setfree("M", SV, v_rc)
                        op("vector", lambda e, pp=p, tt=t0: e.tensor_tensor(
                            out=yhat[0:64, pp, tt:tt + 512],
                            in0=PSY[0:64, 0:512], in1=recmv[0:64, :],
                            op=ALU.mult))
                        v_yh = op("vector", lambda e, pp=p, tt=t0:
                                  e.tensor_tensor(
                                      out=yhat[64:128, pp, tt:tt + 512],
                                      in0=PSY[64:128, 512:1024],
                                      in1=recmv[64:128, :], op=ALU.mult), SV)
                        psm_recmv_cons[0] = v_yh
                        setfree("Y0", SV, v_yh)
                        setfree("Y1", SV, v_yh)

                    eAd = [0, 0]
                    eBd = [0, 0]
                    for i, (tb, jp) in enumerate(units):
                        t0 = tb * 512
                        par = i % 2
                        waitfree("tensor", "A0", "A1")
                        tscA = None
                        for odd in range(2):
                            stile = 2 * jp + odd
                            tscA = op("tensor", lambda e, s_=stile, o=odd,
                                      pp=p, tt=t0, c_=par * 0: e.matmul(
                                          PSA[:, (o + 0) * 512:(o + 1) * 512]
                                          if False else
                                          PSA[:, o * 512:(o + 1) * 512],
                                          ckv[0:64, s_ * 128:(s_ + 1) * 128],
                                          qhalf[0:64, pp, tt:tt + 512],
                                          start=True, stop=True,
                                          tile_position=(0, 0)),
                                      ST if odd == 1 else None)
                        waitfree("tensor", "B0", "B1")
                        tscB = None
                        for odd in range(2):
                            stile = 2 * jp + odd
                            tscB = op("tensor", lambda e, s_=stile, o=odd,
                                      pp=p, tt=t0: e.matmul(
                                          PSB[:, o * 512:(o + 1) * 512],
                                          ckv[64:128, s_ * 128:(s_ + 1) * 128],
                                          qhalf[64:128, pp, tt:tt + 512],
                                          start=True, stop=True,
                                          tile_position=(64, 0)),
                                      ST if odd == 1 else None)
                        if pend is not None:
                            al = emit_attnv(pend)
                            if pend[1] == 7:
                                emit_norm(pend[0], al)
                        wt("scalar", ST, tscA)
                        wt("scalar", SV, negm_ready[p])
                        wt("scalar", expA_free[par][0], expA_free[par][1])
                        eA = op("scalar", lambda e, pa=par, pp=p:
                                e.activation(out=expA[:, pa, :], in_=PSA[:],
                                             func=AF.Exp,
                                             bias=negm_sb[:, 2 * pp:2 * pp + 1],
                                             scale=0.125), SS)
                        setfree("A0", SS, eA)
                        setfree("A1", SS, eA)
                        eAd[par] = eA
                        wt("scalar", ST, tscB)
                        wt("scalar", expB_free[par][0], expB_free[par][1])
                        eB = op("scalar", lambda e, pa=par, pp=p:
                                e.activation(out=expB[:, pa, :], in_=PSB[:],
                                             func=AF.Exp,
                                             bias=negm_sb[:, 2 * pp + 1:
                                                          2 * pp + 2],
                                             scale=0.125), SS)
                        setfree("B0", SS, eB)
                        setfree("B1", SS, eB)
                        eBd[par] = eB
                        pend = (tb, jp)
                    al = emit_attnv(pend)
                    emit_norm(pend[0], al)
                yh_all = SV.n
                if DEBUG and b == 0 and half == 0:
                    wt("sync", SV, yh_all)
                    op("sync", lambda e: e.dma_start(out=taps["tapw"], in_=wvs_sb[:]), SO, 16)
                    op("sync", lambda e: e.dma_start(out=taps["tapyh0"], in_=yhat), SO, 16)
                first = True
                for tb in range(2):
                    for m in range(8):
                        s = (tb * 8 + m) % 4
                        waitfree("tensor", SLOT_RG[s])
                        if first:
                            wt("tensor", SV, yh_all)
                            if beff_done:
                                wt("tensor", SV, beff_done)
                            first = False
                        t = None
                        for k in range(8):
                            t = op("tensor", lambda e, kk=k, mm=m, t2=tb,
                                   ss=s: e.matmul(
                                       SLOT[ss],
                                       wvsv[:, kk, mm * 128:(mm + 1) * 128],
                                       yhat[:, kk, t2 * 512:(t2 + 1) * 512],
                                       start=(kk == 0), stop=(kk == 7)),
                                   ST if k == 7 else None)
                        wt("vector", ST, t)
                        cv = op("vector", lambda e, mm=m, t2=tb, ss=s, h2=hoff:
                                e.scalar_tensor_tensor(
                                    out=x3[:, mm, h2 + t2 * 512:
                                           h2 + (t2 + 1) * 512],
                                    in0=SLOT[ss],
                                    scalar=beff_sb[:, mm:mm + 1],
                                    in1=x3[:, mm, h2 + t2 * 512:
                                           h2 + (t2 + 1) * 512],
                                    op0=ALU.add, op1=ALU.add), SV)
                        setfree(SLOT_RG[s], SV, cv)
                        x_upd[(b, "D", 2 * half + tb)] = (SV, cv)
            wt("gpsimd", ST, ST.n)
            wv_reload[0] = op("gpsimd", lambda e: e.dma_start(
                out=wvsv, in_=d_wv.rearrange("(k p) d -> p k d", p=128)),
                SDW, 16)

        def tap_dump(nm, src_ap, gates):
            if not DEBUG:
                return
            for g in gates:
                if g:
                    wt("sync", g[0], g[1])
            dst = taps[nm]
            if nm in ("tapA", "tapB", "tapCD"):
                dst = dst.rearrange("(ct p) t -> p ct t", p=128)
            op("sync", lambda e, d=dst, s2=src_ap: e.dma_start(out=d, in_=s2),
               SO, 16)

        for b in range(NB):
            ffn_phase(b, is_E=False)
            if b == 0:
                tap_dump("tapA", x3, [x_upd[(0, "A", 3)]])
                tap_dump("tapxr", xr_bf.rearrange("p a b -> p (a b)") if False else xr_bf,
                         [(SV, SV.n)])
                tap_dump("tapxk", xk_bf, [(SV, SV.n)])
            if not SKIP_ATTN:
                beff_done = phase_b(b)
            if DEBUG and b == 0 and not SKIP_ATTN:
                tap_dump("tapB", x3, [(SS, SS.n), (SV, SV.n)])
                tap_dump("tapckv", ckv, [(SS, SS.n)])
                tap_dump("tapcq", cq_bf, [(SS, SS.n)])
            if not SKIP_ATTN:
                phase_cd(b, beff_done)
            if b == 0:
                tap_dump("tapCD", x3, [(SV, SV.n)])
                tap_dump("tapnegm", negm_sb[:], [(SV, SV.n)])
                tap_dump("tapyh", yhat, [(SV, SV.n)])
                tap_dump("tapq", qhalf, [(SS, SS.n)])
            ffn_phase(b, is_E=True)

        wt("sync", SO, SO.n)

        with nc.Block() as block:
            @block.sync
            def _(e):
                for fn in streams["sync"]:
                    fn(e)

            @block.tensor
            def _(e):
                for fn in streams["tensor"]:
                    fn(e)

            @block.vector
            def _(e):
                for fn in streams["vector"]:
                    fn(e)

            @block.scalar
            def _(e):
                for fn in streams["scalar"]:
                    fn(e)

            @block.gpsimd
            def _(e):
                for fn in streams["gpsimd"]:
                    fn(e)

    return nc


def kernel(**inputs):
    g = {k: np.asarray(v) for k, v in inputs.items()}
    x = g["L0_attn_x"]
    xT = np.ascontiguousarray(x.transpose(0, 2, 1)).astype(np.float32, copy=False)
    wkv = g["L0_attn_wkv_d"]
    wkvrT = np.ascontiguousarray(np.concatenate([wkv.T, wkv.T], axis=1))
    bkvr = np.ascontiguousarray(
        np.concatenate([g["L0_attn_bkv_d"], g["L0_attn_bkv_d"]]))
    common = dict(
        wrT=np.ascontiguousarray(g["L0_ffn_W_r"].T),
        wkT=np.ascontiguousarray(g["L0_ffn_W_k"].T),
        wvT=np.ascontiguousarray(g["L0_ffn_W_v"].T),
        wqT=np.ascontiguousarray(g["L0_attn_wq_d"].T),
        wqkT=np.ascontiguousarray(g["L0_attn_w_qk"].T),
        wkvrT=wkvrT,
        wvuT=np.ascontiguousarray(g["L0_attn_wv_u"].T),
        wpjT=np.ascontiguousarray(g["L0_attn_w_proj"].T),
        mur=np.ascontiguousarray(g["L0_ffn_mu_r"].reshape(8, 128).T),
        muk=np.ascontiguousarray(g["L0_ffn_mu_k"].reshape(8, 128).T),
        bq=np.ascontiguousarray(g["L0_attn_bq_d"].reshape(4, 128).T),
        bqk=np.ascontiguousarray(g["L0_attn_b_qk"].reshape(8, 128).T),
        bkvr=np.ascontiguousarray(bkvr.reshape(1, 128).T),
        bvu=np.ascontiguousarray(g["L0_attn_bv_u"].reshape(8, 128).T),
        bpj=np.ascontiguousarray(g["L0_attn_b_proj"].reshape(8, 128).T),
        idn=np.eye(128).astype(ml_dtypes.bfloat16),
    )
    in_maps = [dict(common, xT=np.ascontiguousarray(xT[2 * i:2 * i + 2]))
               for i in range(8)]
    nc = build_program()
    res = run_bass_kernel_spmd(nc, in_maps, core_ids=list(range(8)))
    if DEBUG:
        global TAPS
        TAPS = {k: np.asarray(v) for k, v in res.results[0].items()}
    out = np.concatenate([r["outT"] for r in res.results], axis=0)
    return np.ascontiguousarray(out.transpose(0, 2, 1))


if __name__ == "__main__":
    import reference as R
    ins = R.setup_inputs()
    out = kernel(**{k: np.asarray(v) for k, v in ins.items()})
    ref = np.asarray(R.reference(**{k: np.asarray(v) for k, v in ins.items()}))
    rv = float(np.mean((out - ref) ** 2) / np.mean(ref ** 2))
    print("maxabs", np.abs(out - ref).max(), "resvar", rv)
